# version 01
# baseline (speedup 1.0000x reference)
import numpy as np
import ml_dtypes
import concourse.bass as bass
import concourse.mybir as mybir
from concourse.bass_utils import run_bass_kernel_spmd

F32 = mybir.dt.float32
BF16 = mybir.dt.float16
AF = mybir.ActivationFunctionType
ALU = mybir.AluOpType

D = 1024
H = 16
HD = 64
NCORES = 8
BPC = 4
SEQ = 2048
CT = 128
TOKC = BPC * CT
NCH = SEQ // CT
NKT = D // 128
JTOT = 16384
EPS = 1e-5

ENGINES = ("sync", "tensor", "vector", "scalar", "gpsimd")
DMASEMS = ("dG", "dK", "dAo", "dWa", "dWb", "dCa", "dCb", "dXa", "dXb", "dSa", "dSb",
           "dWr", "dT", "dRa", "dNa", "dNb")

XBF = 0
XN = 4096
SOFF = 8320
MALL = 10368
DSM = 14464
SWC = 17024
SWB = 25216
TR = 41600
SPREV = 74368
SROWB = 74432
LTH = 75456
LTL = 75968
ABF = 76480

QT, KT, VV, OT, KTOK, GATE = TR, TR + 4096, TR + 8192, TR + 12288, TR + 16384, TR + 20480
HT = TR
MIX, GB, RB, KB2_, VB2, ACCB, KVB, WKVB = (TR + i * 4096 for i in range(8))

D_EN, D_E2, D_N, D_NT, D_XA, D_XB, D_GM, D_QD, D_KC, D_EL = (
    0, 128, 256, 384, 512, 576, 640, 768, 896, 960)

FXT = 0
FLS = 8192
FROW = 8960
FCOL = 10496
FDT = 10624
AF32 = 11648

PB0, PB1, PB2, PB3 = 0, 512, 1024, 1536
PB_DS = 2944
PB_O, PB_LT, PB_LST, PB_TOT, PB_FC = 3072, 3200, 3328, 3460, 3468


class Sched:

    def __init__(self):
        self.ops = {e: [] for e in ENGINES}
        self.cnt = {}
        self.lastwait = {}
        self.pending = {e: [] for e in ENGINES}

    def cur(self, sem):
        return self.cnt.get(sem, 0)

    OWN = {"vector": "vE", "scalar": "sE", "gpsimd": "pE"}

    def op(self, eng, fn, waits=(), inc=None, inc_amt=1, dma=False):
        waits = self.pending[eng] + list(waits)
        self.pending[eng] = []
        if fn is not None and not dma and eng in self.OWN:
            own = self.OWN[eng]
            waits.append((own, self.cur(own)))
            inc = own
            inc_amt = 1
        w = []
        for sem, val in waits:
            if val <= 0:
                continue
            assert val <= self.cur(sem), f"wait-for-future {sem} {val} > {self.cur(sem)}"
            key = (eng, sem)
            if self.lastwait.get(key, -1) >= val:
                continue
            self.lastwait[key] = val
            w.append((sem, val))
        if inc is not None:
            self.cnt[inc] = self.cnt.get(inc, 0) + inc_amt
        self.ops[eng].append((w, fn, inc, inc_amt))


class KBX:
    def __init__(self, nc, sched, dbg=()):
        self.nc = nc
        self.s = sched
        self.t = {}
        self.dbg = tuple(dbg)
        self.dbg_t = {}

    def op(self, eng, fn, waits=(), inc=None, inc_amt=1, dma=False):
        self.s.op(eng, fn, waits=waits, inc=inc, inc_amt=inc_amt, dma=dma)
        if inc is not None and not dma and eng in Sched.OWN:
            return self.s.cur(Sched.OWN[eng])
        return self.s.cur(inc) if inc is not None else None

    def dmat(self, queue, out, in_, waits=(), inc="dRa"):
        return self.op(queue, lambda e, o=out, i=in_: e.dma_start_transpose(o, i),
                       waits=waits, inc=inc, inc_amt=16, dma=True)

    def dma(self, queue, out, in_, waits=(), inc="dW", slow=False):
        if slow:
            return self.op(queue, lambda e, o=out, i=in_: e.dma_start(
                out=o, in_=i, allow_slow_non_contiguous=True),
                waits=waits, inc=inc, inc_amt=16, dma=True)
        return self.op(queue, lambda e, o=out, i=in_: e.dma_start(out=o, in_=i),
                       waits=waits, inc=inc, inc_amt=16, dma=True)

    def mm(self, out, lhsT, rhs, start, stop, waits=(), inc=None, tp=None):
        def fn(e, o=out, l=lhsT, r=rhs, st=start, sp=stop, t=tp):
            if t is not None:
                return e.matmul(out=o, lhsT=l, rhs=r, start=st, stop=sp, tile_position=t)
            return e.matmul(out=o, lhsT=l, rhs=r, start=st, stop=sp)
        return self.op("tensor", fn, waits=waits, inc=inc)

    def tr(self, out, in_, ident, waits=(), inc=None):
        return self.op("tensor", lambda e, o=out, i=in_, d=ident:
                       e.transpose(out=o, in_=i, identity=d), waits=waits, inc=inc)

    def act(self, out, in_, func, bias=0.0, scale=1.0, waits=(), inc="sE"):
        return self.op("scalar",
                       lambda e, o=out, i=in_, f=func, b=bias, sc=scale:
                       e.activation(out=o, in_=i, func=f, bias=b, scale=sc),
                       waits=waits, inc=inc)

    def tt(self, eng, out, in0, in1, alu, waits=(), inc=None):
        return self.op(eng, lambda e, o=out, a=in0, b=in1, op=alu:
                       e.tensor_tensor(out=o, in0=a, in1=b, op=op), waits=waits, inc=inc)

    def tss(self, eng, out, in_, scalar, alu, waits=(), inc=None):
        return self.op(eng, lambda e, o=out, a=in_, s=scalar, op=alu:
                       e.tensor_single_scalar(out=o, in_=a, scalar=s, op=op),
                       waits=waits, inc=inc)

    def stt(self, eng, out, in0, scalar, in1, op0, op1, waits=(), inc=None):
        return self.op(eng, lambda e, o=out, a=in0, s=scalar, b=in1, x=op0, y=op1:
                       e.scalar_tensor_tensor(out=o, in0=a, scalar=s, in1=b, op0=x, op1=y),
                       waits=waits, inc=inc)

    def cp(self, eng, out, in_, waits=(), inc=None):
        if eng == "scalar":
            return self.op(eng, lambda e, o=out, i=in_: e.copy(out=o, in_=i),
                           waits=waits, inc=inc)
        return self.op(eng, lambda e, o=out, i=in_: e.tensor_copy(out=o, in_=i),
                       waits=waits, inc=inc)

    def memset(self, eng, ap, val, waits=(), inc=None):
        return self.op(eng, lambda e, a=ap, v=val: e.memset(a, v), waits=waits, inc=inc)

    def fence(self, eng, sems, hard=False):
        w = [(s, self.s.cur(s)) for s in sems]
        if hard:
            self.s.op(eng, None, waits=w)
        else:
            self.s.pending[eng] += w

    def barrier(self):
        allsems = DMASEMS + ("tE", "vE", "sE", "pE")
        for e in ENGINES:
            self.fence(e, allsems)

    def tap(self, name, src_ap, waits):
        import os
        if name not in self.dbg or os.environ.get("TAPS_DECL_ONLY"):
            return
        self.dma("scalar", self.dbg_t[name], src_ap, waits=waits, inc="dT")



def host_constants():
    c = {}
    r = np.arange(128)
    c["c_i128b"] = np.eye(128, dtype=np.float32).astype(np.float16)
    c["c_mcum"] = (r[:, None] <= r[None, :]).astype(np.float32)
    c["c_mrev"] = (r[:, None] > r[None, :]).astype(np.float32)
    c["c_mN0"] = np.where(r[None, :] < r[:, None], -1.0, 0.0).astype(np.float16)
    c["c_mGm"] = np.where(r[:, None] <= r[None, :], 1.0, 0.0).astype(np.float16)
    c["c_ones1x128b"] = np.ones((1, 128), np.float16)
    c["c_meancol"] = (np.ones((128, 1)) / 1024.0).astype(np.float16)
    c["c_onescol"] = np.ones((128, 1), np.float32)
    ind = np.zeros((16, 2048), np.float32)
    for h in range(16):
        ind[h, h * 128:(h + 1) * 128] = 1.0
    c["c_ind16"] = ind.astype(np.float16)
    ind64 = np.zeros((16, 1024), np.float32)
    for q in range(8):
        for p in range(128):
            ind64[2 * q + p // 64, q * 128 + p] = 1.0
    c["c_ind64"] = ind64.astype(np.float16)
    bi = np.zeros((128, 4), np.float32)
    bi[:64, 0] = 1.0
    bi[64:, 1] = 1.0
    bi[:64, 2] = 1.0 / 64
    bi[64:, 3] = 1.0 / 64
    c["c_bi64"] = bi.astype(np.float16)
    rep = np.zeros((4, 128), np.float32)
    rep[0, :64] = 1.0
    rep[1, 64:] = 1.0
    rep[2, :64] = 1.0 / 8
    rep[3, 64:] = 1.0 / 8
    c["c_rep64"] = rep.astype(np.float16)
    c["c_scal"] = np.tile(np.array([[EPS, 1.0, 1e-24, -0.5, -1.0, 0.5, 2.0, 0.0]],
                                   np.float32), (128, 1))
    return c


def host_inputs(inputs):
    x = np.asarray(inputs["L0_attn_a_x"])
    f32 = np.float32
    shared = {}

    def asf(a):
        return np.ascontiguousarray(np.asarray(a), dtype=f32)

    shared["w_aq"] = asf(inputs["L0_attn_a_Wq"])
    shared["w_ak"] = asf(inputs["L0_attn_a_Wk"])
    shared["w_av"] = asf(inputs["L0_attn_a_Wv"])
    shared["w_ao"] = asf(inputs["L0_attn_a_Wo"])
    shared["w_anw"] = asf(inputs["L0_attn_a_norm_w"]).reshape(1024, 1)
    shared["w_afg"] = np.ascontiguousarray(np.concatenate(
        [asf(inputs["L0_attn_a_Wf"]), asf(inputs["L0_attn_a_Wgamma"])], axis=1))
    shared["w_ag1"] = asf(inputs["L0_attn_a_Wg1"])
    shared["w_ag2"] = asf(inputs["L0_attn_a_Wg2"])
    for nm, key in (("w_bg", "L0_attn_b_Wg_w"), ("w_br", "L0_attn_b_Wr_w"),
                    ("w_bk", "L0_attn_b_Wk_w"), ("w_bv", "L0_attn_b_Wv_w"),
                    ("w_bo", "L0_attn_b_Wo_w")):
        shared[nm] = np.ascontiguousarray(asf(inputs[key]).T)
    shared["w_bbias"] = np.ascontiguousarray(np.stack(
        [asf(inputs["L0_attn_b_Wg_b"]), asf(inputs["L0_attn_b_Wr_b"]),
         asf(inputs["L0_attn_b_Wk_b"]), asf(inputs["L0_attn_b_Wv_b"]),
         asf(inputs["L0_attn_b_Wo_b"]),
         asf(inputs["L0_attn_b_mg"]), asf(inputs["L0_attn_b_mr"]),
         asf(inputs["L0_attn_b_mk"]), asf(inputs["L0_attn_b_mv"]),
         asf(inputs["L0_attn_b_w"]).reshape(1024),
         asf(inputs["L0_attn_b_u"]).reshape(1024)], axis=1))

    wc = asf(inputs["L0_ffn_w_cproj"])
    wcr = wc.reshape(1024, 4, 32, 2, 64)
    shared["w_cperm"] = np.ascontiguousarray(
        wcr.transpose(2, 1, 3, 4, 0).reshape(16384, 1024))
    rm = asf(inputs["L0_ffn_rail_maker"])
    rup = asf(inputs["L0_ffn_rail_up_proj"])
    rm2 = np.zeros((128, 4096), f32)
    rup2 = np.zeros((128, 4096), f32)
    for g in range(8):
        for h in range(4):
            for q in range(4):
                for wh in range(2):
                    r = 2 * (4 * g + q) + wh
                    rowb = q * 32 + wh * 16
                    rm2[rowb:rowb + 16, g * 512 + h * 128 + rowb:g * 512 + h * 128 + rowb + 16] =                         rm[h, r].T
                    rup2[rowb:rowb + 16, g * 512 + h * 128 + wh * 64:g * 512 + h * 128 + wh * 64 + 64] =                         rup[h, r].T
    shared["w_rm2"] = rm2
    shared["w_rup2"] = rup2

    consts = host_constants()
    in_maps = []
    for core in range(NCORES):
        xc = np.asarray(x[core * BPC:(core + 1) * BPC])
        xt = np.ascontiguousarray(
            xc.reshape(BPC, NCH, CT, D).transpose(3, 1, 0, 2).reshape(D, NCH * TOKC),
            dtype=f32)
        m = {"xt": xt}
        m.update(shared)
        m.update(consts)
        in_maps.append(m)
    return in_maps


def host_output(results):
    outs = []
    for core in range(NCORES):
        ot = np.asarray(results[core]["outt"])
        outs.append(ot.reshape(D, NCH, BPC, CT).transpose(2, 1, 3, 0).reshape(BPC, SEQ, D))
    return np.ascontiguousarray(np.concatenate(outs, axis=0))


def rr3(ap, p=128):
    return ap.rearrange("(k p) n -> p k n", p=p)



def emit_ln(kb, src, dst, ncols=TOKC, src_is_f32=True, pad=False, ready=(), dstw=None, sqoff=None, pbs=(PB2, PB3), rowo=0):
    T = kb.t
    bA, fA, psA = T["bA"], T["fA"], T["psA"]
    if dstw is None:
        dstw = 516 if pad else ncols
    if sqoff is None:
        assert not pad
        sqoff = dst
    for kt in range(NKT):
        si = (fA if src_is_f32 else bA)[:, src + kt * ncols:src + (kt + 1) * ncols]
        kb.cp("vector", bA[:, XBF + kt * ncols:XBF + (kt + 1) * ncols], si,
              waits=ready if kt == 0 else ())
    for kt in range(NKT):
        kb.tt("vector", bA[:, sqoff + kt * ncols:sqoff + kt * ncols + ncols],
              bA[:, XBF + kt * ncols:XBF + (kt + 1) * ncols],
              bA[:, XBF + kt * ncols:XBF + (kt + 1) * ncols], ALU.mult,
              inc="vE" if kt == NKT - 1 else None)
    v0 = kb.s.cur("vE")
    for kt in range(NKT):
        kb.mm(psA[0:1, pbs[0]:pbs[0] + ncols], T["sc_meancol"][:],
              bA[:, XBF + kt * ncols:XBF + (kt + 1) * ncols],
              start=(kt == 0), stop=(kt == NKT - 1), waits=[("vE", v0)] if kt == 0 else ())
    for kt in range(NKT):
        kb.mm(psA[0:1, pbs[1]:pbs[1] + ncols], T["sc_meancol"][:],
              bA[:, sqoff + kt * ncols:sqoff + kt * ncols + ncols],
              start=(kt == 0), stop=(kt == NKT - 1),
              inc="tE" if kt == NKT - 1 else None)
    t0 = kb.s.cur("tE")
    MU, VAR, TMP = FROW + rowo, FROW + 512 + rowo, FROW + 1024 + rowo
    kb.cp("vector", fA[0:1, MU:MU + ncols], psA[0:1, pbs[0]:pbs[0] + ncols], waits=[("tE", t0)])
    kb.cp("vector", fA[0:1, VAR:VAR + ncols], psA[0:1, pbs[1]:pbs[1] + ncols])
    kb.tt("vector", fA[0:1, TMP:TMP + ncols], fA[0:1, MU:MU + ncols],
          fA[0:1, MU:MU + ncols], ALU.mult)
    kb.tt("vector", fA[0:1, VAR:VAR + ncols], fA[0:1, VAR:VAR + ncols],
          fA[0:1, TMP:TMP + ncols], ALU.subtract, inc="vE")
    v1 = kb.s.cur("vE")
    kb.act(fA[0:1, TMP:TMP + ncols], fA[0:1, VAR:VAR + ncols], AF.Ln, bias=EPS,
           waits=[("vE", v1)])
    kb.act(bA[0:1, SROWB + rowo:SROWB + rowo + ncols], fA[0:1, TMP:TMP + ncols], AF.Exp, scale=-0.5,
           inc="sE")
    s0 = kb.s.cur("sE")
    kb.tt("vector", fA[0:1, TMP:TMP + ncols], fA[0:1, MU:MU + ncols],
          bA[0:1, SROWB + rowo:SROWB + rowo + ncols], ALU.mult, waits=[("sE", s0)])
    kb.cp("vector", bA[0:1, SROWB + 512 + rowo:SROWB + 512 + rowo + ncols], fA[0:1, TMP:TMP + ncols],
          inc="vE")
    v2 = kb.s.cur("vE")
    kb.mm(psA[:, pbs[0]:pbs[0] + ncols], T["sc_ones1x128b"][:], bA[0:1, SROWB + rowo:SROWB + rowo + ncols],
          start=True, stop=True, waits=[("vE", v2)])
    kb.mm(psA[:, pbs[1]:pbs[1] + ncols], T["sc_ones1x128b"][:],
          bA[0:1, SROWB + 512 + rowo:SROWB + 512 + rowo + ncols], start=True, stop=True, inc="tE")
    t1 = kb.s.cur("tE")
    for kt in range(NKT):
        if pad:
            o = bA[:, dst + kt * 516 + 1:dst + kt * 516 + 517].rearrange(
                "p (b t) -> p b t", b=BPC)[:, :, 0:CT]
            a = (fA[:, src + kt * ncols:src + (kt + 1) * ncols] if src_is_f32 else
                 bA[:, XBF + kt * ncols:XBF + (kt + 1) * ncols]).rearrange(
                "p (b t) -> p b t", b=BPC)
            r = psA[:, pbs[0]:pbs[0] + ncols].rearrange("p (b t) -> p b t", b=BPC)
            m = psA[:, pbs[1]:pbs[1] + ncols].rearrange("p (b t) -> p b t", b=BPC)
        else:
            o = bA[:, dst + kt * dstw:dst + kt * dstw + ncols]
            if src_is_f32:
                a = fA[:, src + kt * ncols:src + (kt + 1) * ncols]
            else:
                a = bA[:, XBF + kt * ncols:XBF + (kt + 1) * ncols]
            r = psA[:, pbs[0]:pbs[0] + ncols]
            m = psA[:, pbs[1]:pbs[1] + ncols]
        kb.tt("vector", o, a, r, ALU.mult, waits=[("tE", t1)] if kt == 0 else ())
        kb.tt("vector", o, o, m, ALU.subtract, inc="vE" if kt == NKT - 1 else None)
    return kb.s.cur("vE")



def emit_pretransform(kb, ps=frozenset('abcdefg')):
    T = kb.t
    bA, fA, psA = T["bA"], T["fA"], T["psA"]
    cs = kb.dma("sync", T["sc_scal"][:], T["c_scal"], inc="dK")
    if 'a' in ps:
        for src, dst in (("w_aq", "s_aq"), ("w_ak", "s_ak"), ("w_av", "s_av"),
                         ("w_bg", "s_bg"), ("w_br", "s_br"), ("w_bk", "s_bk"),
                         ("w_bv", "s_bv"), ("w_bo", "s_bo")):
            kb.dma("gpsimd", T[dst], T[src], inc="dG")
        for i in range(8):
            kb.dma("gpsimd", T["s_cperm"][i * 2048:(i + 1) * 2048, :],
                   T["w_cperm"][i * 2048:(i + 1) * 2048, :], inc="dG")
    for nm in ("c_mcum", "c_mrev", "c_i128b", "c_mN0", "c_mGm", "c_ones1x128b",
               "c_meancol", "c_onescol", "c_ind16", "c_ind64", "c_bi64", "c_rep64"):
        kb.dma("sync", T["s" + nm][:], T[nm], inc="dK")
    if 'b' in ps:
        kb.dma("sync", fA[:, FCOL:FCOL + 88], rr3(T["w_bbias"]), inc="dK")
        kb.dma("sync", fA[:, FCOL + 88:FCOL + 96], rr3(T["w_anw"]), inc="dK", slow=True)
        kb.dma("gpsimd", T["safg"][:], rr3(T["w_afg"]), inc="dG")
        kb.dma("gpsimd", T["sag1"][:], rr3(T["w_ag1"]), inc="dG")
        kb.dma("gpsimd", T["sag2"][:], T["w_ag2"], inc="dG")
    srm2 = bA[:, TR:TR + 4096]
    srup2 = bA[:, TR + 4096:TR + 8192]
    if 'b' in ps:
        kb.dma("gpsimd", srm2, T["w_rm2"], inc="dG")
        kb.dma("gpsimd", srup2, T["w_rup2"], inc="dG")
    dgv = kb.s.cur("dG")
    dkv = kb.s.cur("dK")
    for e in ("scalar", "vector", "tensor", "gpsimd"):
        kb.s.pending[e].append(("dK", dkv))

    if 'c' in ps:
        kb.dma("sync", fA[:, FXT:FXT + 8192], rr3(T["w_ao"]), waits=[("dK", dkv)],
               inc="dAo")
        wv = kb.s.cur("dAo")
        for kt in range(NKT):
            kb.tss("vector", bA[:, SWB + kt * 1024:SWB + (kt + 1) * 1024],
                   fA[:, FXT + kt * 1024:FXT + (kt + 1) * 1024],
                   fA[:, FCOL + 88 + kt:FCOL + 89 + kt], ALU.mult,
                   waits=[("dAo", wv)], inc="vE" if kt == NKT - 1 else None)
        v = kb.s.cur("vE")
        kb.dma("sync", rr3(T["s_ao"]), bA[:, SWB:SWB + 8192], waits=[("vE", v)], inc="dAo")

    if 'd' in ps:
        for kt in range(NKT):
            kb.act(fA[:, FDT + kt * 128:FDT + kt * 128 + 1],
                   fA[:, FCOL + kt * 11 + 9:FCOL + kt * 11 + 10], AF.Exp,
                   waits=[("dK", dkv)] if kt == 0 else ())
            kb.act(fA[:, FCOL + 96 + kt:FCOL + 97 + kt],
                   fA[:, FDT + kt * 128:FDT + kt * 128 + 1], AF.Exp, scale=-1.0)
        sv = kb.s.cur("sE")
        for kt in range(NKT):
            kb.memset("vector", fA[:, FDT + kt * 128:FDT + (kt + 1) * 128], 1.0,
                      waits=[("sE", sv)] if kt == 0 else ())
            kb.tss("vector", fA[:, FDT + kt * 128:FDT + (kt + 1) * 128],
                   fA[:, FDT + kt * 128:FDT + (kt + 1) * 128],
                   fA[:, FCOL + 96 + kt:FCOL + 97 + kt], ALU.mult)
    if 'e' in ps:
        kb.memset("vector", bA[:, SOFF:SOFF + 2048], 0.0)
        kb.memset("vector", bA[:, SPREV:SPREV + 64], 0.0, inc="vE")

    mev = 0
    for g in (range(8) if 'f' in ps else []):
        for h in range(4):
            w = [("dG", dgv)] if (g == 0 and h == 0) else []
            if h == 0 and mev:
                w.append(("vE", mev))
            kb.mm(psA[:, PB0 + h * 128:PB0 + (h + 1) * 128],
                  srm2[:, g * 512 + h * 128:g * 512 + (h + 1) * 128],
                  srup2[:, g * 512 + h * 128:g * 512 + (h + 1) * 128],
                  start=True, stop=True, waits=w,
                  inc="tE" if h == 3 else None)
        tv = kb.s.cur("tE")
        mev = kb.cp("vector", bA[:, MALL + g * 512:MALL + (g + 1) * 512],
                    psA[:, PB0:PB0 + 512], waits=[("tE", tv)], inc="vE")
    return kb.s.cur("vE")



def emit_proj(kb, w_off, src_off, ready, evac, G=None):
    bA, psA = kb.t["bA"], kb.t["psA"]
    bankg = G["pbank"] if G is not None else [None, None]
    for ct in range(NKT):
        pb = PB0 if ct % 2 == 0 else PB1
        for kt in range(NKT):
            w = list(ready) if (ct == 0 and kt == 0) else []
            if kt == 0 and bankg[ct % 2]:
                w.append(bankg[ct % 2])
            kb.mm(psA[:, pb:pb + TOKC],
                  bA[:, w_off + kt * 1024 + ct * 128:w_off + kt * 1024 + (ct + 1) * 128],
                  bA[:, src_off + kt * TOKC:src_off + (kt + 1) * TOKC],
                  start=(kt == 0), stop=(kt == NKT - 1), waits=w,
                  inc="tE" if kt == NKT - 1 else None)
        bankg[ct % 2] = evac(ct, psA[:, pb:pb + TOKC], kb.s.cur("tE"))



def emit_attn_a(kb, ch, xoff, G, astop=99):
    import os
    astop = int(os.environ.get("ASTOP", astop))
    T = kb.t
    bA, fA, psA = T["bA"], T["fA"], T["psA"]

    s0, s1 = SWB, SWB + 8192
    wq = kb.dma("sync", bA[:, s0:s0 + 8192], rr3(T["s_aq"]),
                waits=[G["swb"][0], ("dG", G["cast_done"])], inc="dWa")
    wk = kb.dma("sync", bA[:, s1:s1 + 8192], rr3(T["s_ak"]),
                waits=[G["swb"][1]], inc="dWb")

    vln = emit_ln(kb, xoff, XN,
                  ready=[("dXa" if ch % 2 == 0 else "dXb", G["xld"][ch]),
                         ("vE", G["ffn_ev"]), ("sE", G.get("ffn_sE", 0))])
    kb.tap(f"xn1_{ch}", bA[:, XN:XN + 4096], [("vE", vln)])

    if astop < 1:
        return kb.s.cur("vE")
    def ev_q(ct, ps, tv):
        return ("sE", kb.act(bA[:, QT + ct * TOKC:QT + (ct + 1) * TOKC], ps, AF.Silu,
                             waits=[("tE", tv)]))
    emit_proj(kb, s0, XN, [("dWa", wq), ("vE", vln)], ev_q, G)
    q_mm_done = kb.s.cur("tE")
    wv = kb.dma("sync", bA[:, s0:s0 + 8192], rr3(T["s_av"]),
                waits=[("tE", q_mm_done)], inc="dWa")

    if astop < 2:
        return kb.s.cur("vE")
    def ev_k(ct, ps, tv):
        return ("sE", kb.act(bA[:, KT + ct * TOKC:KT + (ct + 1) * TOKC], ps, AF.Silu,
                             waits=[("tE", tv)]))
    emit_proj(kb, s1, XN, [("dWb", wk)], ev_k, G)
    k_mm_done = kb.s.cur("tE")
    wo = kb.dma("sync", bA[:, s1:s1 + 8192], rr3(T["s_ao"]),
                waits=[("tE", k_mm_done), ("dAo", G["ao_ready"])], inc="dWb")
    sil_k = kb.s.cur("sE")

    if astop < 3:
        return kb.s.cur("vE")
    vbg = G["pbank"]
    for b in range(BPC):
        for nh in range(2):
            i2 = (b * 2 + nh) % 2
            pb = PB0 if i2 == 0 else PB1
            for kt in range(NKT):
                w = [("dWa", wv), ("sE", sil_k)] if (b == 0 and nh == 0 and kt == 0) else []
                if kt == 0 and vbg[i2]:
                    w.append(vbg[i2])
                kb.mm(psA[:, pb:pb + 512],
                      bA[:, XN + kt * TOKC + b * CT:XN + kt * TOKC + (b + 1) * CT],
                      bA[:, s0 + kt * 1024 + nh * 512:s0 + kt * 1024 + (nh + 1) * 512],
                      start=(kt == 0), stop=(kt == NKT - 1), waits=w,
                      inc="tE" if kt == NKT - 1 else None)
            vbg[i2] = ("sE", kb.act(
                bA[:, VV + b * 1024 + nh * 512:VV + b * 1024 + (nh + 1) * 512],
                psA[:, pb:pb + 512], AF.Silu, waits=[("tE", kb.s.cur("tE"))]))
    v_done = kb.s.cur("sE")

    if astop < 4:
        return kb.s.cur("vE")
    for kt in range(NKT):
        kb.mm(psA[0:64, PB3:PB3 + TOKC], T["sag1"][:, kt * 64:(kt + 1) * 64],
              bA[:, XN + kt * TOKC:XN + (kt + 1) * TOKC],
              start=(kt == 0), stop=(kt == NKT - 1),
              inc="tE" if kt == NKT - 1 else None)
    tv = kb.s.cur("tE")
    kb.cp("vector", T["st1"][:], psA[0:64, PB3:PB3 + TOKC], waits=[("tE", tv)], inc="vE")
    v_t1 = kb.s.cur("vE")
    gbg = G["pbank"]
    for ct in range(NKT):
        pb = PB0 if ct % 2 == 0 else PB1
        w = [("vE", v_t1), ("sE", v_done)] if ct == 0 else []
        if gbg[ct % 2]:
            w.append(gbg[ct % 2])
        kb.mm(psA[:, pb:pb + TOKC], T["sag2"][:, ct * 128:(ct + 1) * 128], T["st1"][:],
              start=True, stop=True, waits=w, inc="tE")
        gbg[ct % 2] = ("sE", kb.act(
            bA[:, GATE + ct * TOKC:GATE + (ct + 1) * TOKC],
            psA[:, pb:pb + TOKC], AF.Sigmoid, waits=[("tE", kb.s.cur("tE"))]))
    gate_done = kb.s.cur("sE")

    if astop < 5:
        return kb.s.cur("vE")
    for b in range(BPC):
        for kt in range(NKT):
            kb.mm(psA[:, PB2 + b * 32:PB2 + (b + 1) * 32],
                  bA[:, XN + kt * TOKC + b * CT:XN + kt * TOKC + (b + 1) * CT],
                  T["safg"][:, kt * 32:(kt + 1) * 32],
                  start=(kt == 0), stop=(kt == NKT - 1),
                  inc="tE" if kt == NKT - 1 else None)
    tv = kb.s.cur("tE")

    if astop < 6:
        return kb.s.cur("vE")
    for ct in range(NKT):
        pb = PB0 if ct % 2 == 0 else PB1
        pbo = PB1 if ct % 2 == 0 else PB0
        kb.tt("vector", bA[:, XBF + ct * 512:XBF + (ct + 1) * 512],
              bA[:, KT + ct * TOKC:KT + (ct + 1) * TOKC],
              bA[:, KT + ct * TOKC:KT + (ct + 1) * TOKC], ALU.mult,
              waits=[("sE", sil_k), ("vE", vln)] if ct == 0 else (), inc="vE")
        vq = kb.s.cur("vE")
        kb.mm(psA[0:2, pb:pb + 512], T["sc_bi64"][:, 0:2],
              bA[:, XBF + ct * 512:XBF + (ct + 1) * 512], start=True, stop=True,
              waits=[("vE", vq)] + ([("sE", gate_done)] if ct == 0 else []), inc="tE")
        t2 = kb.s.cur("tE")
        kb.act(fA[0:2, FROW:FROW + 512], psA[0:2, pb:pb + 512], AF.Ln, bias=1e-24,
               waits=[("tE", t2)])
        kb.act(bA[0:2, SROWB:SROWB + 512], fA[0:2, FROW:FROW + 512], AF.Exp,
               scale=-0.5, inc="sE")
        s2 = kb.s.cur("sE")
        kb.mm(psA[:, pbo:pbo + 512], T["sc_rep64"][0:2, :], bA[0:2, SROWB:SROWB + 512],
              start=True, stop=True, waits=[("sE", s2)], inc="tE")
        t3 = kb.s.cur("tE")
        kb.tt("vector", bA[:, KT + ct * TOKC:KT + (ct + 1) * TOKC],
              bA[:, KT + ct * TOKC:KT + (ct + 1) * TOKC],
              psA[:, pbo:pbo + 512], ALU.mult, waits=[("tE", t3)], inc="vE")
    k_norm_done = kb.s.cur("vE")
    kb.tap(f"ktn_{ch}", bA[:, KT:KT + 4096], [("vE", k_norm_done)])

    if astop < 7:
        return kb.s.cur("vE")
    FZ, FS, FL, FNL, FR2, FCS, FMB = (FLS, FLS + 128, FLS + 256, FLS + 320,
                                      FLS + 384, FLS + 448, FLS + 512)
    FFC = FLS + 576
    for b in range(BPC):
        z = fA[:, FZ + b * 32:FZ + (b + 1) * 32]
        ls = fA[:, FS + b * 32:FS + (b + 1) * 32]
        kb.cp("vector", z, psA[:, PB2 + b * 32:PB2 + (b + 1) * 32],
              waits=[("tE", tv)], inc="vE")
        vz = kb.s.cur("vE")
        kb.act(fA[:, FROW:FROW + 32], z, AF.Abs, waits=[("vE", vz)])
        kb.act(fA[:, FROW:FROW + 32], fA[:, FROW:FROW + 32], AF.Exp, scale=-1.0)
        kb.act(fA[:, FROW:FROW + 32], fA[:, FROW:FROW + 32], AF.Ln, bias=1.0, inc="sE")
        sz = kb.s.cur("sE")
        kb.tss("vector", ls, z, 0.0, ALU.min, waits=[("sE", sz)])
        kb.tt("vector", ls, ls, fA[:, FROW:FROW + 32], ALU.subtract, inc="vE")
        vls = kb.s.cur("vE")
        lf = fA[:, FS + b * 32:FS + b * 32 + 16]
        kb.mm(psA[:, PB3 + b * 32:PB3 + b * 32 + 16], T["sc_mcum"][:], lf,
              start=True, stop=True, waits=[("vE", vls)])
        kb.mm(psA[:, PB3 + b * 32 + 16:PB3 + b * 32 + 32], T["sc_mrev"][:], lf,
              start=True, stop=True)
        kb.mm(psA[0:16, PB_LT:PB_LT + 128], lf, T["sc_mcum"][:],
              start=True, stop=True)
        kb.mm(psA[0:16, PB_TOT:PB_TOT + 1], lf, T["sc_onescol"][:],
              start=True, stop=True, inc="tE")
        tl = kb.s.cur("tE")
        Lb = fA[:, FL + b * 16:FL + (b + 1) * 16]
        kb.cp("vector", Lb, psA[:, PB3 + b * 32:PB3 + b * 32 + 16], waits=[("tE", tl)])
        kb.tss("vector", fA[:, FNL + b * 16:FNL + (b + 1) * 16], Lb, -1.0, ALU.mult)
        kb.tt("vector", fA[:, FR2 + b * 16:FR2 + (b + 1) * 16], Lb,
              fA[:, FS + b * 32:FS + b * 32 + 16], ALU.add)
        kb.tt("vector", fA[:, FR2 + b * 16:FR2 + (b + 1) * 16],
              fA[:, FR2 + b * 16:FR2 + (b + 1) * 16],
              fA[:, FS + b * 32 + 16:FS + b * 32 + 32], ALU.add, inc="vE")
        vr = kb.s.cur("vE")
        kb.act(fA[:, FCS + b * 16:FCS + (b + 1) * 16],
               psA[:, PB3 + b * 32 + 16:PB3 + b * 32 + 32], AF.Exp, waits=[("vE", vr)])
        kb.act(fA[:, FMB + b * 16:FMB + (b + 1) * 16],
               fA[:, FR2 + b * 16:FR2 + (b + 1) * 16], AF.Exp, inc="sE")
        se = kb.s.cur("sE")
        kb.tss("vector", fA[:, FMB + b * 16:FMB + (b + 1) * 16],
               fA[:, FMB + b * 16:FMB + (b + 1) * 16], -1.0, ALU.mult, waits=[("sE", se)])
        kb.cp("vector", bA[0:16, LTH + b * 128:LTH + (b + 1) * 128],
              psA[0:16, PB_LT:PB_LT + 128])
        kb.tt("vector", bA[0:16, LTL + b * 128:LTL + (b + 1) * 128],
              psA[0:16, PB_LT:PB_LT + 128],
              bA[0:16, LTH + b * 128:LTH + (b + 1) * 128], ALU.subtract, inc="vE")
        kb.act(fA[0:16, FROW:FROW + 1], psA[0:16, PB_TOT:PB_TOT + 1], AF.Exp, inc="sE")
        sf = kb.s.cur("sE")
        kb.cp("vector", bA[0:16, SROWB:SROWB + 1], fA[0:16, FROW:FROW + 1],
              waits=[("sE", sf)], inc="vE")
        vf = kb.s.cur("vE")
        for q in range(8):
            kb.mm(psA[:, PB_FC + q:PB_FC + q + 1], T["sc_ind64"][:, q * 128:(q + 1) * 128],
                  bA[0:16, SROWB:SROWB + 1], start=True, stop=True,
                  waits=[("vE", vf)] if q == 0 else (),
                  inc="tE" if q == 7 else None)
        tf = kb.s.cur("tE")
        kb.cp("vector", fA[:, FFC + b * 8:FFC + (b + 1) * 8],
              psA[:, PB_FC:PB_FC + 8], waits=[("tE", tf)], inc="vE")
    lmach_done = kb.s.cur("vE")
    lmach_se = kb.s.cur("sE")

    if astop < 8:
        return kb.s.cur("vE")
    for b in range(BPC):
        for kt in range(NKT):
            kb.dmat("scalar",
                    bA[:, KTOK + b * 1024 + kt * 128:KTOK + b * 1024 + (kt + 1) * 128],
                    bA[:, KT + kt * TOKC + b * CT:KT + kt * TOKC + (b + 1) * CT],
                    waits=[("vE", k_norm_done)] if (b == 0 and kt == 0) else (),
                    inc="dRa")
    ktok_done = kb.s.cur("dRa")

    if astop < 9:
        return kb.s.cur("vE")
    ZP = {0: (2048, 2176, 3328, 2560, 2432, 2880), 1: (0, 128, 896, 512, 256, 832)}
    inst_guard = {}
    b0_guard = {}
    import os
    dbar = os.environ.get("DBAR")
    dstop = float(os.environ.get("DSTOP", "9"))
    for b in range(BPC):
        for h in range(H):
            if dbar:
                kb.barrier()
            i = b * H + h
            par = i % 2
            st = DSM + par * 1088
            GA, GG, REP, SQ, APP, B0R = ZP[par]
            hb = (h % 2) * 64
            ktc = KT + (h // 2) * TOKC + b * CT
            qtc = QT + (h // 2) * TOKC + b * CT
            scol = SOFF + (b * 8 + h // 2) * 64
            g = inst_guard.get(par)
            w0 = [("vE", k_norm_done), ("vE", lmach_done), ("sE", lmach_se),
                  ("sE", gate_done)]
            if G["psO_guard"]:
                w0.append(("sE", G["psO_guard"]))
            if g:
                w0 = w0 + [("vE", g[0]), ("sE", g[1])]
            kb.mm(psA[:, GA:GA + 128], bA[hb:hb + 64, ktc:ktc + CT],
                  bA[hb:hb + 64, ktc:ktc + CT], start=True, stop=True, waits=w0)
            kb.mm(psA[:, GG:GG + 128], bA[hb:hb + 64, ktc:ktc + CT],
                  bA[hb:hb + 64, qtc:qtc + CT], start=True, stop=True)
            kb.mm(psA[:, REP:REP + 128], T["sc_ind16"][:, h * 128:(h + 1) * 128],
                  bA[0:16, LTH + b * 128:LTH + (b + 1) * 128], start=True, stop=False)
            kb.mm(psA[:, REP:REP + 128], T["sc_ind16"][:, h * 128:(h + 1) * 128],
                  bA[0:16, LTL + b * 128:LTL + (b + 1) * 128], start=False, stop=True,
                  inc="tE")
            t_g = kb.s.cur("tE")
            if dstop < 2:
                inst_guard[par] = (kb.s.cur("vE"), kb.s.cur("sE"))
                continue
            wE = [("tE", t_g)] + ([("vE", g[0])] if g else [])
            kb.op("vector", lambda e, o=psA[:, SQ:SQ + 128], i=psA[:, REP:REP + 128],
                  s1=fA[:, FR2 + b * 16 + h:FR2 + b * 16 + h + 1]:
                  e.tensor_scalar(out=o, in0=i, scalar1=s1, scalar2=0.0,
                                  op0=ALU.subtract, op1=ALU.max), waits=wE)
            kb.op("vector", lambda e, o=psA[:, SQ + 128:SQ + 256], i=psA[:, REP:REP + 128],
                  s1=fA[:, FL + b * 16 + h:FL + b * 16 + h + 1]:
                  e.tensor_scalar(out=o, in0=i, scalar1=s1, scalar2=0.0,
                                  op0=ALU.subtract, op1=ALU.min))
            v_arg = kb.s.cur("vE")
            kb.act(bA[:, st + D_EN:st + D_EN + 128], psA[:, SQ:SQ + 128], AF.Exp,
                   scale=-1.0, waits=[("vE", v_arg)])
            kb.act(bA[:, st + D_E2:st + D_E2 + 128], psA[:, SQ + 128:SQ + 256], AF.Exp)
            kb.act(bA[:, st + D_EL:st + D_EL + 128], psA[:, REP:REP + 128], AF.Exp,
                   inc="sE")
            s_e = kb.s.cur("sE")
            if dstop < 3:
                inst_guard[par] = (kb.s.cur("vE"), kb.s.cur("sE"))
                continue
            kb.tt("vector", bA[:, st + D_N:st + D_N + 128], psA[:, GA:GA + 128],
                  bA[:, st + D_EN:st + D_EN + 128], ALU.mult, waits=[("sE", s_e)])
            kb.tt("vector", bA[:, st + D_N:st + D_N + 128], bA[:, st + D_N:st + D_N + 128],
                  T["sc_mN0"][:], ALU.mult)
            kb.tt("vector", bA[:, st + D_GM:st + D_GM + 128], psA[:, GG:GG + 128],
                  bA[:, st + D_E2:st + D_E2 + 128], ALU.mult)
            kb.tt("vector", bA[:, st + D_GM:st + D_GM + 128],
                  bA[:, st + D_GM:st + D_GM + 128], T["sc_mGm"][:], ALU.mult, inc="vE")
            v_b = kb.s.cur("vE")
            if dstop < 4:
                inst_guard[par] = (kb.s.cur("vE"), kb.s.cur("sE"))
                continue
            kb.tt("vector", bA[hb:hb + 64, st + D_QD:st + D_QD + 128],
                  bA[hb:hb + 64, qtc:qtc + CT],
                  bA[hb:hb + 64, st + D_EL:st + D_EL + 128], ALU.mult)
            kb.tss("vector", bA[:, st + D_KC:st + D_KC + 64],
                   bA[:, KTOK + b * 1024 + h * 64:KTOK + b * 1024 + (h + 1) * 64],
                   fA[:, FCS + b * 16 + h:FCS + b * 16 + h + 1], ALU.mult,
                   waits=[("dRa", ktok_done)], inc="vE")
            v_kc = kb.s.cur("vE")
            if dstop < 5:
                inst_guard[par] = (kb.s.cur("vE"), kb.s.cur("sE"))
                continue
            nsem = "dNa" if par == 0 else "dNb"
            wNT = [("vE", v_b)]
            if G["nt_cons"].get(par):
                wNT.append(("tE", G["nt_cons"][par]))
            kb.dmat("scalar", bA[:, st + D_NT:st + D_NT + 128],
                    bA[:, st + D_N:st + D_N + 128], waits=wNT, inc=nsem)
            s_nt_d = (nsem, kb.s.cur(nsem))
            if dstop < 6:
                inst_guard[par] = (kb.s.cur("vE"), kb.s.cur("sE"))
                continue
            sg = G["sguard"].get((b, h // 2))
            wB = ([("vE", sg)] if sg else []) + \
                 ([("vE", b0_guard[par])] if par in b0_guard else [])
            kb.mm(psA[:, B0R:B0R + 64], bA[hb:hb + 64, ktc:ktc + CT],
                  bA[hb:hb + 64, scol:scol + 64], start=True, stop=True,
                  waits=wB, inc="tE")
            t_b0 = kb.s.cur("tE")
            if dstop < 6.5:
                inst_guard[par] = (kb.s.cur("vE"), kb.s.cur("sE"))
                continue
            kb.stt("vector", bA[:, st + D_XA:st + D_XA + 64], psA[:, B0R:B0R + 64],
                   fA[:, FMB + b * 16 + h:FMB + b * 16 + h + 1],
                   bA[:, VV + b * 1024 + h * 64:VV + b * 1024 + (h + 1) * 64],
                   ALU.mult, ALU.add, waits=[("tE", t_b0), ("sE", v_done)], inc="vE")
            v_w = kb.s.cur("vE")
            b0_guard[par] = v_w
            if dstop < 7:
                inst_guard[par] = (kb.s.cur("vE"), kb.s.cur("sE"))
                continue
            xc, xn_ = D_XA, D_XB
            ev_wait = s_nt_d
            v_app = v_w
            for k in range(7):
                kb.mm(psA[:, APP:APP + 64], bA[:, st + D_NT:st + D_NT + 128],
                      bA[:, st + xc:st + xc + 64], start=True, stop=True,
                      waits=[ev_wait, ("vE", v_app)], inc="tE")
                t_app = kb.s.cur("tE")
                if k < 6:
                    kb.mm(psA[:, SQ:SQ + 128], bA[:, st + D_NT:st + D_NT + 128],
                          bA[:, st + D_N:st + D_N + 128], start=True, stop=True)
                    kb.mm(psA[:, SQ + 128:SQ + 256], bA[:, st + D_N:st + D_N + 128],
                          bA[:, st + D_NT:st + D_NT + 128], start=True, stop=True,
                          inc="tE")
                    t_sq = kb.s.cur("tE")
                kb.tt("vector", bA[:, st + xn_:st + xn_ + 64], bA[:, st + xc:st + xc + 64],
                      psA[:, APP:APP + 64], ALU.add, waits=[("tE", t_app)], inc="vE")
                v_app = kb.s.cur("vE")
                xc, xn_ = xn_, xc
                if k < 6:
                    kb.cp("scalar", bA[:, st + D_N:st + D_N + 256], psA[:, SQ:SQ + 256],
                          waits=[("tE", t_sq)], inc="sE")
                    ev_wait = ("sE", kb.s.cur("sE"))
            U = st + xc
            G["nt_cons"][par] = kb.s.cur("tE")
            if dstop < 8:
                inst_guard[par] = (kb.s.cur("vE"), kb.s.cur("sE"))
                continue
            wO = [("vE", v_app)] + ([("sE", G["psO_guard"])] if G["psO_guard"] else [])
            kb.mm(psA[hb:hb + 64, PB_O:PB_O + 128], bA[:, U:U + 64],
                  bA[:, st + D_GM:st + D_GM + 128], start=True, stop=False, waits=wO)
            kb.mm(psA[hb:hb + 64, PB_O:PB_O + 128], bA[hb:hb + 64, scol:scol + 64],
                  bA[hb:hb + 64, st + D_QD:st + D_QD + 128], start=False, stop=True,
                  inc="tE")
            t_o = kb.s.cur("tE")
            if dstop < 9:
                inst_guard[par] = (kb.s.cur("vE"), kb.s.cur("sE"))
                continue
            wDS = [("vE", v_kc)] + ([("vE", G["ds_guard"])] if G["ds_guard"] else [])
            kb.mm(psA[hb:hb + 64, PB_DS:PB_DS + 64], bA[:, st + D_KC:st + D_KC + 64],
                  bA[:, U:U + 64], start=True, stop=True, waits=wDS, inc="tE")
            t_ds = kb.s.cur("tE")
            if h % 2 == 1:
                kb.cp("scalar",
                      bA[:, OT + (h // 2) * TOKC + b * CT:OT + (h // 2) * TOKC + (b + 1) * CT],
                      psA[:, PB_O:PB_O + 128], waits=[("tE", t_o)], inc="sE")
                G["psO_guard"] = kb.s.cur("sE")
                kb.stt("vector", bA[:, scol:scol + 64], bA[:, scol:scol + 64],
                       fA[:, FFC + b * 8 + h // 2:FFC + b * 8 + h // 2 + 1],
                       psA[:, PB_DS:PB_DS + 64], ALU.mult, ALU.add,
                       waits=[("tE", t_ds)], inc="vE")
                G["sguard"][(b, h // 2)] = kb.s.cur("vE")
                G["ds_guard"] = kb.s.cur("vE")
            inst_guard[par] = (kb.s.cur("vE"), kb.s.cur("sE"))
    delta_done_v = kb.s.cur("vE")
    delta_done_s = kb.s.cur("sE")
    kb.tap(f"ot_{ch}", bA[:, OT:OT + 4096], [("sE", delta_done_s)])

    if astop < 10:
        return kb.s.cur("vE")
    for ct in range(NKT):
        kb.tt("vector", bA[:, OT + ct * TOKC:OT + (ct + 1) * TOKC],
              bA[:, OT + ct * TOKC:OT + (ct + 1) * TOKC],
              bA[:, GATE + ct * TOKC:GATE + (ct + 1) * TOKC], ALU.mult,
              waits=[("sE", max(delta_done_s, gate_done))] if ct == 0 else (),
              inc="vE" if ct == NKT - 1 else None)
    v_g = kb.s.cur("vE")
    gstop = int(os.environ.get("GSTOP", "3"))
    if gstop < 2:
        return kb.s.cur("vE")
    kb.fence("tensor", ["vE", "sE"])
    if os.environ.get("LNINNER", "1") == "0":
        for ct in range(NKT):
            kb.cp("vector", bA[:, GATE + ct * TOKC:GATE + (ct + 1) * TOKC],
                  bA[:, OT + ct * TOKC:OT + (ct + 1) * TOKC],
                  waits=[("vE", v_g)] if ct == 0 else (),
                  inc="vE" if ct == NKT - 1 else None)
        v_n = kb.s.cur("vE")
    else:
        rdy = [("vE", v_g)]
        for gg in G["pbank"]:
            if gg:
                rdy.append(gg)
        v_n = emit_ln(kb, OT, GATE, src_is_f32=False, ready=rdy, pbs=(PB0, PB1))

    def ev_o(ct, ps, tvv):
        kb.tt("vector", fA[:, xoff + ct * 512:xoff + (ct + 1) * 512], ps,
              fA[:, xoff + ct * 512:xoff + (ct + 1) * 512], ALU.add,
              waits=[("tE", tvv)], inc="vE")
        return ("vE", kb.s.cur("vE"))
    if gstop < 3:
        return kb.s.cur("vE")
    emit_proj(kb, s1, GATE, [("dWb", wo), ("vE", v_n)], ev_o, G)
    G["swb"] = [("tE", kb.s.cur("tE")), ("tE", kb.s.cur("tE"))]
    return kb.s.cur("vE")



def emit_ffn(kb, xoff, G, ready=()):
    T = kb.t
    bA, fA, psA = T["bA"], T["fA"], T["psA"]
    kb.fence("tensor", ["vE", "sE", "pE"])
    kb.fence("vector", ["tE", "sE", "pE"])
    vln = emit_ln(kb, xoff, XN, ready=ready)
    ZSL = (1536, 2048, 2560, 3072)
    for th in range(2):
        zguard = G["zslot"]
        for p in range(32):
            q = p % 4
            w = [("vE", vln)] if p == 0 else []
            for h in range(4):
                jt = p * 4 + h
                zg = zguard[q]
                kb.mm(psA[:, ZSL[q] + 0:ZSL[q] + 256],
                      bA[(q) * 32:(q + 1) * 32,
                         MALL + (p // 4) * 512 + h * 128:MALL + (p // 4) * 512 + (h + 1) * 128],
                      bA[(q) * 32:(q + 1) * 32,
                         XN + (p // 4) * TOKC + th * 256:XN + (p // 4) * TOKC + th * 256 + 256],
                      start=True, stop=True, tp=(q * 32, 0),
                      waits=w + ([("sE", zg)] if zg else []), inc="tE")
                tz = kb.s.cur("tE")
                kb.act(bA[:, HT + jt * 256:HT + (jt + 1) * 256],
                       psA[:, ZSL[q] + 0:ZSL[q] + 256], AF.Gelu, waits=[("tE", tz)],
                       inc="sE")
                zguard[q] = kb.s.cur("sE")
                w = []
        rails_done = kb.s.cur("sE")
        for g in range(32):
            slot = SWC + (G["wcn"] % 2) * 4096
            wg = G["swc"][G["wcn"] % 2]
            csem = "dCa" if G["wcn"] % 2 == 0 else "dCb"
            ld = kb.dma("sync", bA[:, slot:slot + 4096],
                        rr3(T["s_cperm"])[:, g * 4:(g + 1) * 4, :],
                        waits=([wg] if wg else []) + ([("dG", G["cast_done"])] if G["wcn"] == 0 else []),
                        inc=csem)
            for jj in range(4):
                js = g * 4 + jj
                for et in range(NKT):
                    kb.mm(psA[:, et * 512:et * 512 + 256],
                          bA[:, slot + jj * 1024 + et * 128:slot + jj * 1024 + (et + 1) * 128],
                          bA[:, HT + js * 256:HT + (js + 1) * 256],
                          start=(g == 0 and jj == 0), stop=(g == 31 and jj == 3),
                          waits=[(csem, ld), ("sE", rails_done),
                                 ("vE", G["ffn_ev"])] if (jj == 0 and et == 0) else (),
                          inc="tE" if (jj == 3 and et == NKT - 1) else None)
            G["swc"][G["wcn"] % 2] = ("tE", kb.s.cur("tE"))
            G["wcn"] += 1
        t_sweep = kb.s.cur("tE")
        for et in range(NKT):
            kb.tt("vector", fA[:, xoff + et * 512 + th * 256:xoff + et * 512 + th * 256 + 256],
                  psA[:, et * 512:et * 512 + 256],
                  fA[:, xoff + et * 512 + th * 256:xoff + et * 512 + th * 256 + 256],
                  ALU.add, waits=[("tE", t_sweep)] if et == 0 else (),
                  inc="vE" if et == NKT - 1 else None)
        G["ffn_ev"] = kb.s.cur("vE")
        kb.fence("tensor", ["vE"])
    G["ffn_sE"] = kb.s.cur("sE")
    return kb.s.cur("vE")



def emit_attn_b(kb, ch, xoff, G):
    import os
    T = kb.t
    bA, fA, psA = T["bA"], T["fA"], T["psA"]
    s0, s1 = SWB, SWB + 8192

    wg_ = kb.dma("sync", bA[:, s0:s0 + 8192], rr3(T["s_bg"]), waits=[G["swb"][0]], inc="dWa")
    wr_ = kb.dma("sync", bA[:, s1:s1 + 8192], rr3(T["s_br"]), waits=[G["swb"][1]], inc="dWb")

    vln = emit_ln(kb, xoff, XN,
                  ready=[("dXa" if ch % 2 == 0 else "dXb", G["xld"][ch]),
                         ("vE", G["mix_done"]),
                         ("vE", G["ffn_ev"]), ("sE", G.get("ffn_sE", 0))],
                  pad=True, sqoff=MIX)
    for kt in range(NKT):
        o = bA[:, XN + kt * 516:XN + kt * 516 + 516].rearrange(
            "p (b t) -> p b t", b=BPC)[:, :, 0:1]
        i = bA[:, SPREV + kt * 4:SPREV + (kt + 1) * 4].rearrange(
            "p (b t) -> p b t", b=BPC, t=1)
        kb.cp("vector", o, i, waits=[("vE", G["lead_v"])] if kt == 0 else (),
              inc="vE" if kt == NKT - 1 else None)
    lead_v = kb.s.cur("vE")

    for kt in range(NKT):
        for b in range(BPC):
            base = XN + kt * 516 + b * 129
            kb.tt("vector", bA[:, XBF + kt * 512 + b * 128:XBF + kt * 512 + (b + 1) * 128],
                  bA[:, base + 1:base + 129], bA[:, base:base + 128], ALU.subtract,
                  waits=[("vE", max(vln, lead_v))] if (kt == 0 and b == 0) else ())
    diff_done = kb.s.cur("vE")

    PIDX = {"g": 5, "r": 6, "k": 7, "v": 8}
    DSTS = {"g": GB, "r": RB, "k": KB2_, "v": VB2}
    mixbufs = {"g": MIX, "r": WKVB, "k": MIX, "v": WKVB}
    wslots = {"g": s0, "r": s1}
    projw = {"g": wg_, "r": wr_}
    prev_mm = {}
    for pi, pn in enumerate(("g", "r", "k", "v")):
        mb = mixbufs[pn]
        if os.environ.get("MIXBAR"):
            kb.barrier()
        mix_w = [("vE", diff_done)]
        if pi >= 2:
            mix_w.append(("tE", prev_mm[("g", "r", "k", "v")[pi - 2]]))
        for kt in range(NKT):
            for b in range(BPC):
                base = XN + kt * 516 + b * 129
                kb.stt("vector", bA[:, mb + kt * 512 + b * 128:mb + kt * 512 + (b + 1) * 128],
                       bA[:, XBF + kt * 512 + b * 128:XBF + kt * 512 + (b + 1) * 128],
                       fA[:, FCOL + kt * 11 + PIDX[pn]:FCOL + kt * 11 + PIDX[pn] + 1],
                       bA[:, base:base + 128], ALU.mult, ALU.add,
                       waits=mix_w if (kt == 0 and b == 0) else ())
        mx = kb.s.cur("vE")
        if pn == "k":
            projw["k"] = kb.dma("sync", bA[:, s0:s0 + 8192], rr3(T["s_bk"]),
                                waits=[("tE", prev_mm["g"])], inc="dWa")
            wslots["k"] = s0
        if pn == "v":
            projw["v"] = kb.dma("sync", bA[:, s1:s1 + 8192], rr3(T["s_bv"]),
                                waits=[("tE", prev_mm["r"])], inc="dWb")
            wslots["v"] = s1
        dst = DSTS[pn]
        bj = PIDX[pn] - 5
        for ct in range(NKT):
            pb = PB0 if ct % 2 == 0 else PB1
            for kt in range(NKT):
                wsem = {"g": "dWa", "r": "dWb", "k": "dWa", "v": "dWb"}[pn]
                w = [(wsem, projw[pn]), ("vE", mx)] if (ct == 0 and kt == 0) else []
                if kt == 0 and G["pbank"][ct % 2]:
                    w.append(G["pbank"][ct % 2])
                kb.mm(psA[:, pb:pb + TOKC],
                      bA[:, wslots[pn] + kt * 1024 + ct * 128:wslots[pn] + kt * 1024 + (ct + 1) * 128],
                      bA[:, mb + kt * TOKC:mb + (kt + 1) * TOKC],
                      start=(kt == 0), stop=(kt == NKT - 1), waits=w,
                      inc="tE" if kt == NKT - 1 else None)
            tvv = kb.s.cur("tE")
            bcol = fA[:, FCOL + ct * 11 + bj:FCOL + ct * 11 + bj + 1]
            od = bA[:, dst + ct * TOKC:dst + (ct + 1) * TOKC]
            if pn == "g":
                G["pbank"][ct % 2] = ("sE", kb.act(od, psA[:, pb:pb + TOKC], AF.Sigmoid,
                                                   bias=bcol, waits=[("tE", tvv)]))
            else:
                kb.tss("vector", od, psA[:, pb:pb + TOKC], bcol, ALU.add,
                       waits=[("tE", tvv)], inc="vE")
                G["pbank"][ct % 2] = ("vE", kb.s.cur("vE"))
        prev_mm[pn] = kb.s.cur("tE")
    proj_v = kb.s.cur("vE")
    proj_s = kb.s.cur("sE")
    G["mix_done"] = kb.s.cur("vE")

    if ch + 1 < NCH:
        for kt in range(NKT):
            o = bA[:, SPREV + kt * 4:SPREV + (kt + 1) * 4].rearrange(
                "p (b t) -> p b t", b=BPC, t=1)
            i = bA[:, XN + kt * 516:XN + kt * 516 + 516].rearrange(
                "p (b t) -> p b t", b=BPC)[:, :, 128:129]
            kb.cp("vector", o, i,
                  waits=[("vE", G["mix_done"])] if kt == 0 else ())
        G["lead_v"] = kb.s.cur("vE")

    for ct in range(NKT):
        kb.tt("vector", bA[:, KVB + ct * TOKC:KVB + (ct + 1) * TOKC],
              bA[:, KB2_ + ct * TOKC:KB2_ + (ct + 1) * TOKC],
              bA[:, VB2 + ct * TOKC:VB2 + (ct + 1) * TOKC], ALU.mult,
              waits=[("vE", proj_v)] if ct == 0 else (),
              inc="vE" if ct == NKT - 1 else None)
    v_kv = kb.s.cur("vE")
    for ct in range(NKT):
        for b in range(BPC):
            o = bA[:, ACCB + ct * TOKC + b * CT:ACCB + ct * TOKC + (b + 1) * CT]
            init = 0.0 if ch == 0 else bA[:, SPREV + 32 + ct * 4 + b:
                                          SPREV + 32 + ct * 4 + b + 1]
            kb.op("vector", lambda e, o=o, d0=fA[:, FDT + ct * 128:FDT + (ct + 1) * 128],
                  d1=bA[:, KVB + ct * TOKC + b * CT:KVB + ct * TOKC + (b + 1) * CT],
                  ini=init: e.tensor_tensor_scan(out=o, data0=d0, data1=d1, initial=ini,
                                                 op0=ALU.mult, op1=ALU.add),
                  waits=[("vE", v_kv)] if (ct == 0 and b == 0) else (),
                  inc="vE" if (ct == NKT - 1 and b == BPC - 1) else None)
    v_scan = kb.s.cur("vE")
    for ct in range(NKT):
        o = bA[:, SPREV + 32 + ct * 4:SPREV + 32 + (ct + 1) * 4].rearrange(
            "p (b t) -> p b t", b=BPC, t=1)
        i = bA[:, ACCB + ct * TOKC:ACCB + (ct + 1) * TOKC].rearrange(
            "p (b t) -> p b t", b=BPC)[:, :, 127:128]
        kb.cp("vector", o, i, inc="vE" if ct == NKT - 1 else None)
    for ct in range(NKT):
        kb.stt("vector", bA[:, WKVB + ct * TOKC:WKVB + (ct + 1) * TOKC],
               bA[:, KVB + ct * TOKC:KVB + (ct + 1) * TOKC],
               fA[:, FCOL + ct * 11 + 10:FCOL + ct * 11 + 11],
               bA[:, ACCB + ct * TOKC:ACCB + (ct + 1) * TOKC], ALU.mult, ALU.add,
               waits=[("vE", v_scan)] if ct == 0 else ())
        kb.tt("vector", bA[:, MIX + ct * TOKC:MIX + (ct + 1) * TOKC],
              bA[:, RB + ct * TOKC:RB + (ct + 1) * TOKC],
              bA[:, WKVB + ct * TOKC:WKVB + (ct + 1) * TOKC], ALU.mult)
        kb.tt("vector", bA[:, WKVB + ct * TOKC:WKVB + (ct + 1) * TOKC],
              bA[:, MIX + ct * TOKC:MIX + (ct + 1) * TOKC],
              bA[:, MIX + ct * TOKC:MIX + (ct + 1) * TOKC], ALU.mult,
              inc="vE" if ct == NKT - 1 else None)
    v_rw = kb.s.cur("vE")
    for ct in range(NKT):
        pb = PB0 if ct % 2 == 0 else PB1
        pbo = PB1 if ct % 2 == 0 else PB0
        w = [("vE", v_rw)] if ct == 0 else []
        for gg in G["pbank"]:
            if gg:
                w.append(gg)
        kb.mm(psA[0:2, pb:pb + 512], T["sc_bi64"][:, 2:4],
              bA[:, MIX + ct * TOKC:MIX + (ct + 1) * TOKC], start=True, stop=True,
              waits=w)
        kb.mm(psA[0:2, pbo:pbo + 512], T["sc_bi64"][:, 2:4],
              bA[:, WKVB + ct * TOKC:WKVB + (ct + 1) * TOKC], start=True, stop=True,
              inc="tE")
        tg = kb.s.cur("tE")
        kb.cp("vector", fA[0:2, FROW:FROW + 512], psA[0:2, pb:pb + 512], waits=[("tE", tg)])
        kb.cp("vector", fA[0:2, FROW + 512:FROW + 1024], psA[0:2, pbo:pbo + 512])
        kb.tt("vector", fA[0:2, FROW + 1024:FROW + 1536], fA[0:2, FROW:FROW + 512],
              fA[0:2, FROW:FROW + 512], ALU.mult)
        kb.tt("vector", fA[0:2, FROW + 512:FROW + 1024], fA[0:2, FROW + 512:FROW + 1024],
              fA[0:2, FROW + 1024:FROW + 1536], ALU.subtract)
        kb.cp("vector", bA[0:2, SROWB + 512:SROWB + 1024], fA[0:2, FROW:FROW + 512],
              inc="vE")
        vg = kb.s.cur("vE")
        kb.act(fA[0:2, FROW + 1024:FROW + 1536], fA[0:2, FROW + 512:FROW + 1024],
               AF.Ln, bias=EPS, waits=[("vE", vg)])
        kb.act(bA[0:2, SROWB:SROWB + 512], fA[0:2, FROW + 1024:FROW + 1536],
               AF.Exp, scale=-0.5, inc="sE")
        sg = kb.s.cur("sE")
        kb.mm(psA[:, pbo:pbo + 512], T["sc_rep64"][0:2, :], bA[0:2, SROWB:SROWB + 512],
              start=True, stop=True, waits=[("sE", sg)])
        kb.mm(psA[:, pb:pb + 512], T["sc_rep64"][0:2, :],
              bA[0:2, SROWB + 512:SROWB + 1024], start=True, stop=True, inc="tE")
        tg2 = kb.s.cur("tE")
        kb.tt("vector", bA[:, KVB + ct * TOKC:KVB + (ct + 1) * TOKC],
              bA[:, MIX + ct * TOKC:MIX + (ct + 1) * TOKC],
              psA[:, pb:pb + 512], ALU.subtract, waits=[("tE", tg2)])
        kb.tt("vector", bA[:, KVB + ct * TOKC:KVB + (ct + 1) * TOKC],
              bA[:, KVB + ct * TOKC:KVB + (ct + 1) * TOKC],
              psA[:, pbo:pbo + 512], ALU.mult)
        kb.tt("vector", bA[:, GB + ct * TOKC:GB + (ct + 1) * TOKC],
              bA[:, GB + ct * TOKC:GB + (ct + 1) * TOKC],
              bA[:, KVB + ct * TOKC:KVB + (ct + 1) * TOKC], ALU.mult,
              waits=[("sE", proj_s)] if ct == 0 else (), inc="vE")
        G["pbank"][ct % 2] = ("vE", kb.s.cur("vE"))
    v_pre = kb.s.cur("vE")
    wo_ = kb.dma("sync", bA[:, s0:s0 + 8192], rr3(T["s_bo"]),
                 waits=[("tE", prev_mm["k"])], inc="dWa")

    def ev_o(ct, ps, tvv):
        kb.stt("vector", fA[:, xoff + ct * 512:xoff + (ct + 1) * 512], ps,
               fA[:, FCOL + ct * 11 + 4:FCOL + ct * 11 + 5],
               fA[:, xoff + ct * 512:xoff + (ct + 1) * 512], ALU.add, ALU.add,
               waits=[("tE", tvv)], inc="vE")
        return ("vE", kb.s.cur("vE"))
    emit_proj(kb, s0, GB, [("dWa", wo_), ("vE", v_pre)], ev_o, G)
    G["swb"] = [("tE", kb.s.cur("tE")), ("tE", kb.s.cur("tE"))]
    return kb.s.cur("vE")



def build_full(dbg=(), p1=NCH, p2=NCH, sub=(1, 1, 1, 1)):
    nc = bass.Bass("TRN2", target_bir_lowering=False)
    sched = Sched()
    kb = KBX(nc, sched, dbg)
    T = kb.t

    def din(name, shape, dt=F32):
        T[name] = nc.dram_tensor(name, list(shape), dt, kind="ExternalInput").ap()

    din("xt", (D, NCH * TOKC))
    for nm in ("w_aq", "w_ak", "w_av", "w_ao"):
        din(nm, (D, D))
    din("w_anw", (D, 1)); din("w_afg", (D, 32)); din("w_ag1", (D, 64)); din("w_ag2", (64, D))
    for nm in ("w_bg", "w_br", "w_bk", "w_bv", "w_bo"):
        din(nm, (D, D))
    din("w_bbias", (D, 11))
    din("w_cperm", (JTOT, D))
    din("w_rm2", (128, 4096)); din("w_rup2", (128, 4096))
    din("c_mcum", (128, 128)); din("c_mrev", (128, 128))
    din("c_i128b", (128, 128), BF16); din("c_mN0", (128, 128), BF16)
    din("c_mGm", (128, 128), BF16)
    din("c_ones1x128b", (1, 128), BF16); din("c_meancol", (128, 1), BF16)
    din("c_onescol", (128, 1)); din("c_ind16", (16, 2048), BF16)
    din("c_ind64", (16, 1024), BF16); din("c_bi64", (128, 4), BF16)
    din("c_rep64", (4, 128), BF16); din("c_scal", (128, 8))
    T["outt"] = nc.dram_tensor("outt", [D, NCH * TOKC], F32, kind="ExternalOutput").ap()
    T["x2t"] = nc.dram_tensor("x2t", [D, NCH * TOKC], F32, kind="Internal").ap()
    for nm in ("s_aq", "s_ak", "s_av", "s_ao", "s_bg", "s_br", "s_bk", "s_bv", "s_bo"):
        T[nm] = nc.dram_tensor(nm, [D, D], BF16, kind="Internal").ap()
    T["s_cperm"] = nc.dram_tensor("s_cperm", [JTOT, D], BF16, kind="Internal").ap()
    for nm in dbg:
        shp = [D, NCH * TOKC] if nm == "x2t" else [128, 4096]
        kb.dbg_t[nm] = nc.dram_tensor("dbg_" + nm, shp, F32,
                                      kind="ExternalOutput").ap()

    import contextlib
    with contextlib.ExitStack() as stack:
        ent = stack.enter_context
        bA_h = ent(nc.sbuf_tensor("bA", [128, ABF], BF16))
        fA_h = ent(nc.sbuf_tensor("fA", [128, AF32], F32))
        consts = {}
        for nm, shape, dt in (("sc_mcum", [128, 128], F32), ("sc_mrev", [128, 128], F32),
                              ("sc_i128b", [128, 128], BF16), ("sc_mN0", [128, 128], BF16),
                              ("sc_mGm", [128, 128], BF16), ("sc_ones1x128b", [1, 128], BF16),
                              ("sc_meancol", [128, 1], BF16), ("sc_onescol", [128, 1], F32),
                              ("sc_ind16", [16, 2048], BF16), ("sc_ind64", [16, 1024], BF16),
                              ("sc_bi64", [128, 4], BF16), ("sc_rep64", [4, 128], BF16),
                              ("safg", [128, 256], BF16), ("sag1", [128, 512], BF16),
                              ("sag2", [64, 1024], BF16),
                              ("st1", [64, 512], BF16), ("sc_scal", [128, 8], F32)):
            consts[nm] = ent(nc.sbuf_tensor(nm, shape, dt))
        psA_h = ent(nc.psum_tensor("psA", [128, 4096], F32))
        sems = {}
        for snm in DMASEMS + ("tE", "vE", "sE", "pE"):
            sems[snm] = ent(nc.semaphore(snm))
        block = ent(nc.Block())
        T.update(consts)
        T["bA"] = bA_h
        T["fA"] = fA_h
        T["psA"] = psA_h
        bA, fA = T["bA"], T["fA"]

        sc = T["sc_scal"]
        for i, v in enumerate([EPS, 1.0, 1e-24, -0.5, -1.0, 0.5, 2.0, 0.0]):
            nc.const_aps.register(F32, float(v), sc[:, i:i + 1])
        G = {"swb": [None, None], "swc": [None, None], "wcn": 0, "pbank": [None, None],
             "ffn_sE": 0,
             "zslot": [None] * 4, "ffn_ev": 0, "sguard": {}, "psO_guard": 0,
             "ds_guard": 0, "nt_cons": {},
             "xld": {}, "xstore": {}, "mix_done": 0, "lead_v": 0}

        import os
        pret_v = emit_pretransform(kb, frozenset(os.environ.get("PSTAGES", "abcdefg")))
        G["cast_done"] = kb.s.cur("dG")
        G["ao_ready"] = kb.s.cur("dAo")
        G["swb"] = [("dAo", G["ao_ready"]), ("dAo", G["ao_ready"])]

        xt3 = rr3(T["xt"])
        x23 = rr3(T["x2t"])
        ot3 = rr3(T["outt"])

        def xsem(c):
            return "dXa" if c % 2 == 0 else "dXb"

        def ssem(c):
            return "dSa" if c % 2 == 0 else "dSb"

        G["xld"][0] = kb.dma("scalar", fA[:, FXT:FXT + 4096],
                             xt3[:, :, 0:TOKC], waits=[("dAo", G["ao_ready"])], inc="dXa")
        for ch in range(p1):
            xoff = FXT + (ch % 2) * 4096
            if ch + 1 < p1:
                w = [("dAo", G["ao_ready"])] if ch == 0 else \
                    [(ssem(ch - 1), G["xstore"].get(ch - 1, 0))]
                G["xld"][ch + 1] = kb.dma(
                    "scalar", fA[:, FXT + ((ch + 1) % 2) * 4096:FXT + ((ch + 1) % 2) * 4096 + 4096],
                    xt3[:, :, (ch + 1) * TOKC:(ch + 2) * TOKC], waits=w, inc=xsem(ch + 1))
            kb.barrier()
            if sub[0]:
                emit_attn_a(kb, ch, xoff, G)
            kb.tap(f"x1_{ch}", fA[:, xoff:xoff + 4096], [("vE", kb.s.cur("vE"))])
            v2 = kb.s.cur("vE")
            if sub[1]:
                kb.barrier()
                v2 = emit_ffn(kb, xoff, G,
                              ready=[("dXa" if ch % 2 == 0 else "dXb", G["xld"][ch])])
            kb.tap(f"x2_{ch}", fA[:, xoff:xoff + 4096], [("vE", v2)])
            G["xstore"][ch] = kb.dma("scalar", x23[:, :, ch * TOKC:(ch + 1) * TOKC],
                                     fA[:, xoff:xoff + 4096],
                                     waits=[("vE", v2)], inc=ssem(ch))

        if "x2t" in dbg:
            kb.dma("gpsimd", kb.dbg_t["x2t"][:, 0:p1 * TOKC], T["x2t"][:, 0:p1 * TOKC],
                   waits=[("dSa", kb.s.cur("dSa")), ("dSb", kb.s.cur("dSb"))], inc="dT")
        if p2 == 0:
            kb.fence("sync", list(DMASEMS), hard=True)
            _emit(sched, sems, block)
            return nc
        wst = G["xstore"][p1 - 1]
        for b in range(BPC):
            col = (p1 - 1) * TOKC + b * CT + CT - 1
            kb.dma("scalar",
                   fA[:, FLS + 640:FLS + 672].rearrange("p (k b) -> p k b", b=BPC)[:, :, b:b + 1],
                   x23[:, :, col:col + 1],
                   waits=[(ssem(p1 - 1), wst)] if b == 0 else [], inc="dWr", slow=True)
        wld = kb.s.cur("dWr")
        G["lead_v"] = emit_ln(kb, FLS + 640, SPREV, ncols=4, dstw=4,
                              ready=[("dWr", wld), ("vE", G["ffn_ev"]),
                                     ("sE", G.get("ffn_sE", 0))])

        G["xld"] = {}
        ws0 = G["xstore"].get(p1 - 2, 0)
        G["xld"][0] = kb.dma("scalar", fA[:, FXT:FXT + 4096], x23[:, :, 0:TOKC],
                             waits=[(ssem(p1 - 2), ws0)] if ws0 else [], inc="dXa")
        G["xstore"] = {}
        for ch in range(p2):
            xoff = FXT + (ch % 2) * 4096
            if ch + 1 < p2:
                w = [(ssem(ch - 1), G["xstore"].get(ch - 1, 0))] if ch >= 1 else []
                G["xld"][ch + 1] = kb.dma(
                    "scalar", fA[:, FXT + ((ch + 1) % 2) * 4096:FXT + ((ch + 1) % 2) * 4096 + 4096],
                    x23[:, :, (ch + 1) * TOKC:(ch + 2) * TOKC], waits=w, inc=xsem(ch + 1))
            kb.barrier()
            if sub[2]:
                emit_attn_b(kb, ch, xoff, G)
            kb.tap(f"x3_{ch}", fA[:, xoff:xoff + 4096], [("vE", kb.s.cur("vE"))])
            v2 = kb.s.cur("vE")
            if sub[3]:
                kb.barrier()
                v2 = emit_ffn(kb, xoff, G,
                              ready=[("dXa" if ch % 2 == 0 else "dXb", G["xld"][ch])])
            G["xstore"][ch] = kb.dma("scalar", ot3[:, :, ch * TOKC:(ch + 1) * TOKC],
                                     fA[:, xoff:xoff + 4096],
                                     waits=[("vE", v2)], inc=ssem(ch))

        kb.fence("sync", list(DMASEMS), hard=True)
        _emit(sched, sems, block)
    return nc


def _emit(sched, sems, block):
    def body(eng_name):
        def run(eng):
            for w, fn, inc, amt in sched.ops[eng_name]:
                for sem, val in w:
                    eng.wait_ge(sems[sem], val)
                if fn is None:
                    continue
                ins = fn(eng)
                if inc is not None:
                    ins.then_inc(sems[inc], amt)
        return run
    block.sync(body("sync"))
    block.tensor(body("tensor"))
    block.vector(body("vector"))
    block.scalar(body("scalar"))
    block.gpsimd(body("gpsimd"))


_CACHE = {}


def kernel(**inputs):
    in_maps = host_inputs(inputs)
    if "nc" not in _CACHE:
        _CACHE["nc"] = build_full()
    nc = _CACHE["nc"]
    res = run_bass_kernel_spmd(nc, in_maps, core_ids=list(range(NCORES)))
    return host_output(res.results)


# version 03 (final)
# speedup vs baseline: 1.2028x; 1.2028x over previous
import numpy as np
import ml_dtypes
import concourse.bass as bass
import concourse.mybir as mybir
from concourse.bass_utils import run_bass_kernel_spmd

F32 = mybir.dt.float32
BF16 = mybir.dt.float16
AF = mybir.ActivationFunctionType
ALU = mybir.AluOpType

D = 1024
H = 16
HD = 64
NCORES = 8
BPC = 4
SEQ = 2048
CT = 128
TOKC = BPC * CT
NCH = SEQ // CT
NKT = D // 128
JTOT = 16384
EPS = 1e-5

ENGINES = ("sync", "tensor", "vector", "scalar", "gpsimd")
DMASEMS = ("dG", "dK", "dAo", "dWa", "dWb", "dCa", "dCb", "dXa", "dXb", "dSa", "dSb",
           "dWr", "dT", "dRa", "dNa", "dNb")

import os as _os
WQQ = _os.environ.get("WQQ", "gpsimd")
DMATQ = _os.environ.get("DMATQ", "sync")

XBF = 0
XN = 4096
SOFF = 8320
MALL = 10368
DSM = 14464
SWC = 17024
SWB = 25216
TR = 41600
SPREV = 74368
SROWB = 74432
LTH = 75456
LTL = 75968
ABF = 76480

QT, KT, VV, OT, KTOK, GATE = TR, TR + 4096, TR + 8192, TR + 12288, TR + 16384, TR + 20480
HT = TR
MIX, GB, RB, KB2_, VB2, ACCB, KVB, WKVB = (TR + i * 4096 for i in range(8))

D_EN, D_E2, D_N, D_NT, D_XA, D_XB, D_GM, D_QD, D_KC, D_EL = (
    0, 128, 256, 384, 512, 576, 640, 768, 896, 960)

FXT = 0
FLS = 8192
FROW = 8960
FCOL = 10496
FDT = 10624
AF32 = 11648

PB0, PB1, PB2, PB3 = 0, 512, 1024, 1536
PB_DS = 2944
PB_O, PB_LT, PB_LST, PB_TOT, PB_FC = 3072, 3200, 3328, 3460, 3468


class Sched:

    def __init__(self):
        self.ops = {e: [] for e in ENGINES}
        self.cnt = {}
        self.lastwait = {}
        self.pending = {e: [] for e in ENGINES}

    def cur(self, sem):
        return self.cnt.get(sem, 0)

    OWN = {"vector": "vE", "scalar": "sE", "gpsimd": "pE"}

    def op(self, eng, fn, waits=(), inc=None, inc_amt=1, dma=False):
        waits = self.pending[eng] + list(waits)
        self.pending[eng] = []
        if fn is not None and not dma and eng in self.OWN:
            own = self.OWN[eng]
            waits.append((own, self.cur(own)))
            inc = own
            inc_amt = 1
        w = []
        for sem, val in waits:
            if val <= 0:
                continue
            assert val <= self.cur(sem), f"wait-for-future {sem} {val} > {self.cur(sem)}"
            key = (eng, sem)
            if self.lastwait.get(key, -1) >= val:
                continue
            self.lastwait[key] = val
            w.append((sem, val))
        if inc is not None:
            self.cnt[inc] = self.cnt.get(inc, 0) + inc_amt
        self.ops[eng].append((w, fn, inc, inc_amt))


class KBX:
    def __init__(self, nc, sched, dbg=()):
        self.nc = nc
        self.s = sched
        self.t = {}
        self.dbg = tuple(dbg)
        self.dbg_t = {}

    def op(self, eng, fn, waits=(), inc=None, inc_amt=1, dma=False):
        self.s.op(eng, fn, waits=waits, inc=inc, inc_amt=inc_amt, dma=dma)
        if inc is not None and not dma and eng in Sched.OWN:
            return self.s.cur(Sched.OWN[eng])
        return self.s.cur(inc) if inc is not None else None

    def dmat(self, queue, out, in_, waits=(), inc="dRa"):
        return self.op(queue, lambda e, o=out, i=in_: e.dma_start_transpose(o, i),
                       waits=waits, inc=inc, inc_amt=16, dma=True)

    def dma(self, queue, out, in_, waits=(), inc="dW", slow=False):
        if slow:
            return self.op(queue, lambda e, o=out, i=in_: e.dma_start(
                out=o, in_=i, allow_slow_non_contiguous=True),
                waits=waits, inc=inc, inc_amt=16, dma=True)
        return self.op(queue, lambda e, o=out, i=in_: e.dma_start(out=o, in_=i),
                       waits=waits, inc=inc, inc_amt=16, dma=True)

    def mm(self, out, lhsT, rhs, start, stop, waits=(), inc=None, tp=None):
        def fn(e, o=out, l=lhsT, r=rhs, st=start, sp=stop, t=tp):
            if t is not None:
                return e.matmul(out=o, lhsT=l, rhs=r, start=st, stop=sp, tile_position=t)
            return e.matmul(out=o, lhsT=l, rhs=r, start=st, stop=sp)
        return self.op("tensor", fn, waits=waits, inc=inc)

    def tr(self, out, in_, ident, waits=(), inc=None):
        return self.op("tensor", lambda e, o=out, i=in_, d=ident:
                       e.transpose(out=o, in_=i, identity=d), waits=waits, inc=inc)

    def act(self, out, in_, func, bias=0.0, scale=1.0, waits=(), inc="sE"):
        return self.op("scalar",
                       lambda e, o=out, i=in_, f=func, b=bias, sc=scale:
                       e.activation(out=o, in_=i, func=f, bias=b, scale=sc),
                       waits=waits, inc=inc)

    def tt(self, eng, out, in0, in1, alu, waits=(), inc=None):
        return self.op(eng, lambda e, o=out, a=in0, b=in1, op=alu:
                       e.tensor_tensor(out=o, in0=a, in1=b, op=op), waits=waits, inc=inc)

    def tss(self, eng, out, in_, scalar, alu, waits=(), inc=None):
        return self.op(eng, lambda e, o=out, a=in_, s=scalar, op=alu:
                       e.tensor_single_scalar(out=o, in_=a, scalar=s, op=op),
                       waits=waits, inc=inc)

    def stt(self, eng, out, in0, scalar, in1, op0, op1, waits=(), inc=None):
        return self.op(eng, lambda e, o=out, a=in0, s=scalar, b=in1, x=op0, y=op1:
                       e.scalar_tensor_tensor(out=o, in0=a, scalar=s, in1=b, op0=x, op1=y),
                       waits=waits, inc=inc)

    def cp(self, eng, out, in_, waits=(), inc=None):
        if eng == "scalar":
            return self.op(eng, lambda e, o=out, i=in_: e.copy(out=o, in_=i),
                           waits=waits, inc=inc)
        return self.op(eng, lambda e, o=out, i=in_: e.tensor_copy(out=o, in_=i),
                       waits=waits, inc=inc)

    def memset(self, eng, ap, val, waits=(), inc=None):
        return self.op(eng, lambda e, a=ap, v=val: e.memset(a, v), waits=waits, inc=inc)

    def fence(self, eng, sems, hard=False):
        w = [(s, self.s.cur(s)) for s in sems]
        if hard:
            self.s.op(eng, None, waits=w)
        else:
            self.s.pending[eng] += w

    def barrier(self):
        allsems = DMASEMS + ("tE", "vE", "sE", "pE")
        for e in ENGINES:
            self.fence(e, allsems)

    def tap(self, name, src_ap, waits):
        import os
        if name not in self.dbg or os.environ.get("TAPS_DECL_ONLY"):
            return
        self.dma("scalar", self.dbg_t[name], src_ap, waits=waits, inc="dT")



def host_constants():
    c = {}
    r = np.arange(128)
    c["c_i128b"] = np.eye(128, dtype=np.float32).astype(np.float16)
    c["c_mcum"] = (r[:, None] <= r[None, :]).astype(np.float32)
    c["c_mrev"] = (r[:, None] > r[None, :]).astype(np.float32)
    c["c_mN0"] = np.where(r[None, :] < r[:, None], -1.0, 0.0).astype(np.float16)
    c["c_mGm"] = np.where(r[:, None] <= r[None, :], 1.0, 0.0).astype(np.float16)
    c["c_ones1x128b"] = np.ones((1, 128), np.float16)
    c["c_meancol"] = (np.ones((128, 1)) / 1024.0).astype(np.float16)
    c["c_onescol"] = np.ones((128, 1), np.float32)
    ind = np.zeros((16, 2048), np.float32)
    for h in range(16):
        ind[h, h * 128:(h + 1) * 128] = 1.0
    c["c_ind16"] = ind.astype(np.float16)
    ind64 = np.zeros((16, 1024), np.float32)
    for q in range(8):
        for p in range(128):
            ind64[2 * q + p // 64, q * 128 + p] = 1.0
    c["c_ind64"] = ind64.astype(np.float16)
    bi = np.zeros((128, 4), np.float32)
    bi[:64, 0] = 1.0
    bi[64:, 1] = 1.0
    bi[:64, 2] = 1.0 / 64
    bi[64:, 3] = 1.0 / 64
    c["c_bi64"] = bi.astype(np.float16)
    rep = np.zeros((4, 128), np.float32)
    rep[0, :64] = 1.0
    rep[1, 64:] = 1.0
    rep[2, :64] = 1.0 / 8
    rep[3, 64:] = 1.0 / 8
    c["c_rep64"] = rep.astype(np.float16)
    c["c_scal"] = np.tile(np.array([[EPS, 1.0, 1e-24, -0.5, -1.0, 0.5, 2.0, 0.0]],
                                   np.float32), (128, 1))
    return c


def host_inputs(inputs):
    x = np.asarray(inputs["L0_attn_a_x"])
    f32 = np.float32
    shared = {}

    def asf(a):
        return np.ascontiguousarray(np.asarray(a), dtype=f32)

    shared["w_aq"] = asf(inputs["L0_attn_a_Wq"])
    shared["w_ak"] = asf(inputs["L0_attn_a_Wk"])
    shared["w_av"] = asf(inputs["L0_attn_a_Wv"])
    shared["w_ao"] = asf(inputs["L0_attn_a_Wo"])
    shared["w_anw"] = asf(inputs["L0_attn_a_norm_w"]).reshape(1024, 1)
    shared["w_afg"] = np.ascontiguousarray(np.concatenate(
        [asf(inputs["L0_attn_a_Wf"]), asf(inputs["L0_attn_a_Wgamma"])], axis=1))
    shared["w_ag1"] = asf(inputs["L0_attn_a_Wg1"])
    shared["w_ag2"] = asf(inputs["L0_attn_a_Wg2"])
    for nm, key in (("w_bg", "L0_attn_b_Wg_w"), ("w_br", "L0_attn_b_Wr_w"),
                    ("w_bk", "L0_attn_b_Wk_w"), ("w_bv", "L0_attn_b_Wv_w"),
                    ("w_bo", "L0_attn_b_Wo_w")):
        shared[nm] = np.ascontiguousarray(asf(inputs[key]).T)
    shared["w_bbias"] = np.ascontiguousarray(np.stack(
        [asf(inputs["L0_attn_b_Wg_b"]), asf(inputs["L0_attn_b_Wr_b"]),
         asf(inputs["L0_attn_b_Wk_b"]), asf(inputs["L0_attn_b_Wv_b"]),
         asf(inputs["L0_attn_b_Wo_b"]),
         asf(inputs["L0_attn_b_mg"]), asf(inputs["L0_attn_b_mr"]),
         asf(inputs["L0_attn_b_mk"]), asf(inputs["L0_attn_b_mv"]),
         asf(inputs["L0_attn_b_w"]).reshape(1024),
         asf(inputs["L0_attn_b_u"]).reshape(1024)], axis=1))

    wc = asf(inputs["L0_ffn_w_cproj"])
    wcr = wc.reshape(1024, 4, 32, 2, 64)
    shared["w_cperm"] = np.ascontiguousarray(
        wcr.transpose(2, 1, 3, 4, 0).reshape(16384, 1024))
    rm = asf(inputs["L0_ffn_rail_maker"])
    rup = asf(inputs["L0_ffn_rail_up_proj"])
    rm2 = np.zeros((128, 4096), f32)
    rup2 = np.zeros((128, 4096), f32)
    for g in range(8):
        for h in range(4):
            for q in range(4):
                for wh in range(2):
                    r = 2 * (4 * g + q) + wh
                    rowb = q * 32 + wh * 16
                    rm2[rowb:rowb + 16, g * 512 + h * 128 + rowb:g * 512 + h * 128 + rowb + 16] =                         rm[h, r].T
                    rup2[rowb:rowb + 16, g * 512 + h * 128 + wh * 64:g * 512 + h * 128 + wh * 64 + 64] =                         rup[h, r].T
    shared["w_rm2"] = rm2
    shared["w_rup2"] = rup2

    consts = host_constants()
    in_maps = []
    for core in range(NCORES):
        xc = np.asarray(x[core * BPC:(core + 1) * BPC])
        xt = np.ascontiguousarray(
            xc.reshape(BPC, NCH, CT, D).transpose(3, 1, 0, 2).reshape(D, NCH * TOKC),
            dtype=f32)
        m = {"xt": xt}
        m.update(shared)
        m.update(consts)
        in_maps.append(m)
    return in_maps


def host_output(results):
    outs = []
    for core in range(NCORES):
        ot = np.asarray(results[core]["outt"])
        outs.append(ot.reshape(D, NCH, BPC, CT).transpose(2, 1, 3, 0).reshape(BPC, SEQ, D))
    return np.ascontiguousarray(np.concatenate(outs, axis=0))


def rr3(ap, p=128):
    return ap.rearrange("(k p) n -> p k n", p=p)



def emit_ln(kb, src, dst, ncols=TOKC, src_is_f32=True, pad=False, ready=(), dstw=None, sqoff=None, pbs=(PB2, PB3), rowo=0):
    T = kb.t
    bA, fA, psA = T["bA"], T["fA"], T["psA"]
    if dstw is None:
        dstw = 516 if pad else ncols
    if sqoff is None:
        assert not pad
        sqoff = dst
    for kt in range(NKT):
        si = (fA if src_is_f32 else bA)[:, src + kt * ncols:src + (kt + 1) * ncols]
        kb.cp("vector", bA[:, XBF + kt * ncols:XBF + (kt + 1) * ncols], si,
              waits=ready if kt == 0 else ())
    for kt in range(NKT):
        kb.tt("vector", bA[:, sqoff + kt * ncols:sqoff + kt * ncols + ncols],
              bA[:, XBF + kt * ncols:XBF + (kt + 1) * ncols],
              bA[:, XBF + kt * ncols:XBF + (kt + 1) * ncols], ALU.mult,
              inc="vE" if kt == NKT - 1 else None)
    v0 = kb.s.cur("vE")
    for kt in range(NKT):
        kb.mm(psA[0:1, pbs[0]:pbs[0] + ncols], T["sc_meancol"][:],
              bA[:, XBF + kt * ncols:XBF + (kt + 1) * ncols],
              start=(kt == 0), stop=(kt == NKT - 1), waits=[("vE", v0)] if kt == 0 else ())
    for kt in range(NKT):
        kb.mm(psA[0:1, pbs[1]:pbs[1] + ncols], T["sc_meancol"][:],
              bA[:, sqoff + kt * ncols:sqoff + kt * ncols + ncols],
              start=(kt == 0), stop=(kt == NKT - 1),
              inc="tE" if kt == NKT - 1 else None)
    t0 = kb.s.cur("tE")
    MU, VAR, TMP = FROW + rowo, FROW + 512 + rowo, FROW + 1024 + rowo
    kb.cp("vector", fA[0:1, MU:MU + ncols], psA[0:1, pbs[0]:pbs[0] + ncols], waits=[("tE", t0)])
    kb.cp("vector", fA[0:1, VAR:VAR + ncols], psA[0:1, pbs[1]:pbs[1] + ncols])
    kb.tt("vector", fA[0:1, TMP:TMP + ncols], fA[0:1, MU:MU + ncols],
          fA[0:1, MU:MU + ncols], ALU.mult)
    kb.tt("vector", fA[0:1, VAR:VAR + ncols], fA[0:1, VAR:VAR + ncols],
          fA[0:1, TMP:TMP + ncols], ALU.subtract, inc="vE")
    v1 = kb.s.cur("vE")
    kb.act(fA[0:1, TMP:TMP + ncols], fA[0:1, VAR:VAR + ncols], AF.Ln, bias=EPS,
           waits=[("vE", v1)])
    kb.act(bA[0:1, SROWB + rowo:SROWB + rowo + ncols], fA[0:1, TMP:TMP + ncols], AF.Exp, scale=-0.5,
           inc="sE")
    s0 = kb.s.cur("sE")
    kb.tt("vector", fA[0:1, TMP:TMP + ncols], fA[0:1, MU:MU + ncols],
          bA[0:1, SROWB + rowo:SROWB + rowo + ncols], ALU.mult, waits=[("sE", s0)])
    kb.cp("vector", bA[0:1, SROWB + 512 + rowo:SROWB + 512 + rowo + ncols], fA[0:1, TMP:TMP + ncols],
          inc="vE")
    v2 = kb.s.cur("vE")
    kb.mm(psA[:, pbs[0]:pbs[0] + ncols], T["sc_ones1x128b"][:], bA[0:1, SROWB + rowo:SROWB + rowo + ncols],
          start=True, stop=True, waits=[("vE", v2)])
    kb.mm(psA[:, pbs[1]:pbs[1] + ncols], T["sc_ones1x128b"][:],
          bA[0:1, SROWB + 512 + rowo:SROWB + 512 + rowo + ncols], start=True, stop=True, inc="tE")
    t1 = kb.s.cur("tE")
    for kt in range(NKT):
        if pad:
            o = bA[:, dst + kt * 516 + 1:dst + kt * 516 + 517].rearrange(
                "p (b t) -> p b t", b=BPC)[:, :, 0:CT]
            a = (fA[:, src + kt * ncols:src + (kt + 1) * ncols] if src_is_f32 else
                 bA[:, XBF + kt * ncols:XBF + (kt + 1) * ncols]).rearrange(
                "p (b t) -> p b t", b=BPC)
            r = psA[:, pbs[0]:pbs[0] + ncols].rearrange("p (b t) -> p b t", b=BPC)
            m = psA[:, pbs[1]:pbs[1] + ncols].rearrange("p (b t) -> p b t", b=BPC)
        else:
            o = bA[:, dst + kt * dstw:dst + kt * dstw + ncols]
            if src_is_f32:
                a = fA[:, src + kt * ncols:src + (kt + 1) * ncols]
            else:
                a = bA[:, XBF + kt * ncols:XBF + (kt + 1) * ncols]
            r = psA[:, pbs[0]:pbs[0] + ncols]
            m = psA[:, pbs[1]:pbs[1] + ncols]
        kb.tt("vector", o, a, r, ALU.mult, waits=[("tE", t1)] if kt == 0 else ())
        kb.tt("vector", o, o, m, ALU.subtract, inc="vE" if kt == NKT - 1 else None)
    return kb.s.cur("vE")



def emit_pretransform(kb, ps=frozenset('abcdefg')):
    T = kb.t
    bA, fA, psA = T["bA"], T["fA"], T["psA"]
    cs = kb.dma("sync", T["sc_scal"][:], T["c_scal"], inc="dK")
    if 'a' in ps:
        for src, dst in (("w_aq", "s_aq"), ("w_ak", "s_ak"), ("w_av", "s_av"),
                         ("w_bg", "s_bg"), ("w_br", "s_br"), ("w_bk", "s_bk"),
                         ("w_bv", "s_bv"), ("w_bo", "s_bo")):
            kb.dma("gpsimd", T[dst], T[src], inc="dG")
        for i in range(8):
            kb.dma("gpsimd", T["s_cperm"][i * 2048:(i + 1) * 2048, :],
                   T["w_cperm"][i * 2048:(i + 1) * 2048, :], inc="dG")
    for nm in ("c_mcum", "c_mrev", "c_i128b", "c_mN0", "c_mGm", "c_ones1x128b",
               "c_meancol", "c_onescol", "c_ind16", "c_ind64", "c_bi64", "c_rep64"):
        kb.dma("sync", T["s" + nm][:], T[nm], inc="dK")
    if 'b' in ps:
        kb.dma("sync", fA[:, FCOL:FCOL + 88], rr3(T["w_bbias"]), inc="dK")
        kb.dma("sync", fA[:, FCOL + 88:FCOL + 96], rr3(T["w_anw"]), inc="dK", slow=True)
        kb.dma("gpsimd", T["safg"][:], rr3(T["w_afg"]), inc="dG")
        kb.dma("gpsimd", T["sag1"][:], rr3(T["w_ag1"]), inc="dG")
        kb.dma("gpsimd", T["sag2"][:], T["w_ag2"], inc="dG")
    srm2 = bA[:, TR:TR + 4096]
    srup2 = bA[:, TR + 4096:TR + 8192]
    if 'b' in ps:
        kb.dma("gpsimd", srm2, T["w_rm2"], inc="dG")
        kb.dma("gpsimd", srup2, T["w_rup2"], inc="dG")
    dgv = kb.s.cur("dG")
    dkv = kb.s.cur("dK")
    for e in ("scalar", "vector", "tensor", "gpsimd"):
        kb.s.pending[e].append(("dK", dkv))

    if 'c' in ps:
        kb.dma("sync", fA[:, FXT:FXT + 8192], rr3(T["w_ao"]), waits=[("dK", dkv)],
               inc="dAo")
        wv = kb.s.cur("dAo")
        for kt in range(NKT):
            kb.tss("vector", bA[:, SWB + kt * 1024:SWB + (kt + 1) * 1024],
                   fA[:, FXT + kt * 1024:FXT + (kt + 1) * 1024],
                   fA[:, FCOL + 88 + kt:FCOL + 89 + kt], ALU.mult,
                   waits=[("dAo", wv)], inc="vE" if kt == NKT - 1 else None)
        v = kb.s.cur("vE")
        kb.dma("sync", rr3(T["s_ao"]), bA[:, SWB:SWB + 8192], waits=[("vE", v)], inc="dAo")

    if 'd' in ps:
        for kt in range(NKT):
            kb.act(fA[:, FDT + kt * 128:FDT + kt * 128 + 1],
                   fA[:, FCOL + kt * 11 + 9:FCOL + kt * 11 + 10], AF.Exp,
                   waits=[("dK", dkv)] if kt == 0 else ())
            kb.act(fA[:, FCOL + 96 + kt:FCOL + 97 + kt],
                   fA[:, FDT + kt * 128:FDT + kt * 128 + 1], AF.Exp, scale=-1.0)
        sv = kb.s.cur("sE")
        for kt in range(NKT):
            kb.memset("vector", fA[:, FDT + kt * 128:FDT + (kt + 1) * 128], 1.0,
                      waits=[("sE", sv)] if kt == 0 else ())
            kb.tss("vector", fA[:, FDT + kt * 128:FDT + (kt + 1) * 128],
                   fA[:, FDT + kt * 128:FDT + (kt + 1) * 128],
                   fA[:, FCOL + 96 + kt:FCOL + 97 + kt], ALU.mult)
    if 'e' in ps:
        kb.memset("vector", bA[:, SOFF:SOFF + 2048], 0.0)
        kb.memset("vector", bA[:, SPREV:SPREV + 64], 0.0, inc="vE")

    mev = 0
    for g in (range(8) if 'f' in ps else []):
        for h in range(4):
            w = [("dG", dgv)] if (g == 0 and h == 0) else []
            if h == 0 and mev:
                w.append(("vE", mev))
            kb.mm(psA[:, PB0 + h * 128:PB0 + (h + 1) * 128],
                  srm2[:, g * 512 + h * 128:g * 512 + (h + 1) * 128],
                  srup2[:, g * 512 + h * 128:g * 512 + (h + 1) * 128],
                  start=True, stop=True, waits=w,
                  inc="tE" if h == 3 else None)
        tv = kb.s.cur("tE")
        mev = kb.cp("vector", bA[:, MALL + g * 512:MALL + (g + 1) * 512],
                    psA[:, PB0:PB0 + 512], waits=[("tE", tv)], inc="vE")
    return kb.s.cur("vE")



def emit_proj(kb, w_off, src_off, ready, evac, G=None):
    bA, psA = kb.t["bA"], kb.t["psA"]
    bankg = G["pbank"] if G is not None else [None, None]
    for ct in range(NKT):
        pb = PB0 if ct % 2 == 0 else PB1
        for kt in range(NKT):
            w = list(ready) if (ct == 0 and kt == 0) else []
            if kt == 0 and bankg[ct % 2]:
                w.append(bankg[ct % 2])
            kb.mm(psA[:, pb:pb + TOKC],
                  bA[:, w_off + kt * 1024 + ct * 128:w_off + kt * 1024 + (ct + 1) * 128],
                  bA[:, src_off + kt * TOKC:src_off + (kt + 1) * TOKC],
                  start=(kt == 0), stop=(kt == NKT - 1), waits=w,
                  inc="tE" if kt == NKT - 1 else None)
        bankg[ct % 2] = evac(ct, psA[:, pb:pb + TOKC], kb.s.cur("tE"))



def emit_attn_a(kb, ch, xoff, G, astop=99):
    import os
    astop = int(os.environ.get("ASTOP", astop))
    T = kb.t
    bA, fA, psA = T["bA"], T["fA"], T["psA"]

    s0, s1 = SWB, SWB + 8192
    wq = kb.dma(WQQ, bA[:, s0:s0 + 8192], rr3(T["s_aq"]),
                waits=[G["swb"][0], ("dG", G["cast_done"])], inc="dWa")
    wk = kb.dma(WQQ, bA[:, s1:s1 + 8192], rr3(T["s_ak"]),
                waits=[G["swb"][1]], inc="dWb")

    vln = emit_ln(kb, xoff, XN,
                  ready=[("dXa" if ch % 2 == 0 else "dXb", G["xld"][ch]),
                         ("vE", G["ffn_ev"]), ("sE", G.get("ffn_sE", 0))])
    kb.tap(f"xn1_{ch}", bA[:, XN:XN + 4096], [("vE", vln)])

    if astop < 1:
        return kb.s.cur("vE")
    def ev_q(ct, ps, tv):
        return ("sE", kb.act(bA[:, QT + ct * TOKC:QT + (ct + 1) * TOKC], ps, AF.Silu,
                             waits=[("tE", tv)]))
    emit_proj(kb, s0, XN, [("dWa", wq), ("vE", vln)], ev_q, G)
    q_mm_done = kb.s.cur("tE")
    wv = kb.dma(WQQ, bA[:, s0:s0 + 8192], rr3(T["s_av"]),
                waits=[("tE", q_mm_done)], inc="dWa")

    if astop < 2:
        return kb.s.cur("vE")
    def ev_k(ct, ps, tv):
        return ("sE", kb.act(bA[:, KT + ct * TOKC:KT + (ct + 1) * TOKC], ps, AF.Silu,
                             waits=[("tE", tv)]))
    emit_proj(kb, s1, XN, [("dWb", wk)], ev_k, G)
    k_mm_done = kb.s.cur("tE")
    wo = kb.dma(WQQ, bA[:, s1:s1 + 8192], rr3(T["s_ao"]),
                waits=[("tE", k_mm_done), ("dAo", G["ao_ready"])], inc="dWb")
    sil_k = kb.s.cur("sE")

    if astop < 3:
        return kb.s.cur("vE")
    vbg = G["pbank"]
    for b in range(BPC):
        for nh in range(2):
            i2 = (b * 2 + nh) % 2
            pb = PB0 if i2 == 0 else PB1
            for kt in range(NKT):
                w = [("dWa", wv), ("sE", sil_k)] if (b == 0 and nh == 0 and kt == 0) else []
                if kt == 0 and vbg[i2]:
                    w.append(vbg[i2])
                kb.mm(psA[:, pb:pb + 512],
                      bA[:, XN + kt * TOKC + b * CT:XN + kt * TOKC + (b + 1) * CT],
                      bA[:, s0 + kt * 1024 + nh * 512:s0 + kt * 1024 + (nh + 1) * 512],
                      start=(kt == 0), stop=(kt == NKT - 1), waits=w,
                      inc="tE" if kt == NKT - 1 else None)
            vbg[i2] = ("sE", kb.act(
                bA[:, VV + b * 1024 + nh * 512:VV + b * 1024 + (nh + 1) * 512],
                psA[:, pb:pb + 512], AF.Silu, waits=[("tE", kb.s.cur("tE"))]))
    v_done = kb.s.cur("sE")

    if astop < 4:
        return kb.s.cur("vE")
    for kt in range(NKT):
        kb.mm(psA[0:64, PB3:PB3 + TOKC], T["sag1"][:, kt * 64:(kt + 1) * 64],
              bA[:, XN + kt * TOKC:XN + (kt + 1) * TOKC],
              start=(kt == 0), stop=(kt == NKT - 1),
              inc="tE" if kt == NKT - 1 else None)
    tv = kb.s.cur("tE")
    kb.cp("vector", T["st1"][:], psA[0:64, PB3:PB3 + TOKC], waits=[("tE", tv)], inc="vE")
    v_t1 = kb.s.cur("vE")
    gbg = G["pbank"]
    for ct in range(NKT):
        pb = PB0 if ct % 2 == 0 else PB1
        w = [("vE", v_t1), ("sE", v_done)] if ct == 0 else []
        if gbg[ct % 2]:
            w.append(gbg[ct % 2])
        kb.mm(psA[:, pb:pb + TOKC], T["sag2"][:, ct * 128:(ct + 1) * 128], T["st1"][:],
              start=True, stop=True, waits=w, inc="tE")
        gbg[ct % 2] = ("sE", kb.act(
            bA[:, GATE + ct * TOKC:GATE + (ct + 1) * TOKC],
            psA[:, pb:pb + TOKC], AF.Sigmoid, waits=[("tE", kb.s.cur("tE"))]))
    gate_done = kb.s.cur("sE")

    if astop < 5:
        return kb.s.cur("vE")
    for b in range(BPC):
        for kt in range(NKT):
            kb.mm(psA[:, PB2 + b * 32:PB2 + (b + 1) * 32],
                  bA[:, XN + kt * TOKC + b * CT:XN + kt * TOKC + (b + 1) * CT],
                  T["safg"][:, kt * 32:(kt + 1) * 32],
                  start=(kt == 0), stop=(kt == NKT - 1),
                  inc="tE" if kt == NKT - 1 else None)
    tv = kb.s.cur("tE")

    if astop < 6:
        return kb.s.cur("vE")
    for ct in range(NKT):
        pb = PB0 if ct % 2 == 0 else PB1
        pbo = PB1 if ct % 2 == 0 else PB0
        kb.tt("vector", bA[:, XBF + ct * 512:XBF + (ct + 1) * 512],
              bA[:, KT + ct * TOKC:KT + (ct + 1) * TOKC],
              bA[:, KT + ct * TOKC:KT + (ct + 1) * TOKC], ALU.mult,
              waits=[("sE", sil_k), ("vE", vln)] if ct == 0 else (), inc="vE")
        vq = kb.s.cur("vE")
        kb.mm(psA[0:2, pb:pb + 512], T["sc_bi64"][:, 0:2],
              bA[:, XBF + ct * 512:XBF + (ct + 1) * 512], start=True, stop=True,
              waits=[("vE", vq)] + ([("sE", gate_done)] if ct == 0 else []), inc="tE")
        t2 = kb.s.cur("tE")
        kb.act(fA[0:2, FROW:FROW + 512], psA[0:2, pb:pb + 512], AF.Ln, bias=1e-24,
               waits=[("tE", t2)])
        kb.act(bA[0:2, SROWB:SROWB + 512], fA[0:2, FROW:FROW + 512], AF.Exp,
               scale=-0.5, inc="sE")
        s2 = kb.s.cur("sE")
        kb.mm(psA[:, pbo:pbo + 512], T["sc_rep64"][0:2, :], bA[0:2, SROWB:SROWB + 512],
              start=True, stop=True, waits=[("sE", s2)], inc="tE")
        t3 = kb.s.cur("tE")
        kb.tt("vector", bA[:, KT + ct * TOKC:KT + (ct + 1) * TOKC],
              bA[:, KT + ct * TOKC:KT + (ct + 1) * TOKC],
              psA[:, pbo:pbo + 512], ALU.mult, waits=[("tE", t3)], inc="vE")
    k_norm_done = kb.s.cur("vE")
    kb.tap(f"ktn_{ch}", bA[:, KT:KT + 4096], [("vE", k_norm_done)])

    if astop < 7:
        return kb.s.cur("vE")
    FZ, FS, FL, FNL, FR2, FCS, FMB = (FLS, FLS + 128, FLS + 256, FLS + 320,
                                      FLS + 384, FLS + 448, FLS + 512)
    FFC = FLS + 576
    for b in range(BPC):
        z = fA[:, FZ + b * 32:FZ + (b + 1) * 32]
        ls = fA[:, FS + b * 32:FS + (b + 1) * 32]
        kb.cp("vector", z, psA[:, PB2 + b * 32:PB2 + (b + 1) * 32],
              waits=[("tE", tv)], inc="vE")
        vz = kb.s.cur("vE")
        kb.act(fA[:, FROW:FROW + 32], z, AF.Abs, waits=[("vE", vz)])
        kb.act(fA[:, FROW:FROW + 32], fA[:, FROW:FROW + 32], AF.Exp, scale=-1.0)
        kb.act(fA[:, FROW:FROW + 32], fA[:, FROW:FROW + 32], AF.Ln, bias=1.0, inc="sE")
        sz = kb.s.cur("sE")
        kb.tss("vector", ls, z, 0.0, ALU.min, waits=[("sE", sz)])
        kb.tt("vector", ls, ls, fA[:, FROW:FROW + 32], ALU.subtract, inc="vE")
        vls = kb.s.cur("vE")
        lf = fA[:, FS + b * 32:FS + b * 32 + 16]
        kb.mm(psA[:, PB3 + b * 32:PB3 + b * 32 + 16], T["sc_mcum"][:], lf,
              start=True, stop=True, waits=[("vE", vls)])
        kb.mm(psA[:, PB3 + b * 32 + 16:PB3 + b * 32 + 32], T["sc_mrev"][:], lf,
              start=True, stop=True)
        kb.mm(psA[0:16, PB_LT:PB_LT + 128], lf, T["sc_mcum"][:],
              start=True, stop=True)
        kb.mm(psA[0:16, PB_TOT:PB_TOT + 1], lf, T["sc_onescol"][:],
              start=True, stop=True, inc="tE")
        tl = kb.s.cur("tE")
        Lb = fA[:, FL + b * 16:FL + (b + 1) * 16]
        kb.cp("vector", Lb, psA[:, PB3 + b * 32:PB3 + b * 32 + 16], waits=[("tE", tl)])
        kb.tss("vector", fA[:, FNL + b * 16:FNL + (b + 1) * 16], Lb, -1.0, ALU.mult)
        kb.tt("vector", fA[:, FR2 + b * 16:FR2 + (b + 1) * 16], Lb,
              fA[:, FS + b * 32:FS + b * 32 + 16], ALU.add)
        kb.tt("vector", fA[:, FR2 + b * 16:FR2 + (b + 1) * 16],
              fA[:, FR2 + b * 16:FR2 + (b + 1) * 16],
              fA[:, FS + b * 32 + 16:FS + b * 32 + 32], ALU.add, inc="vE")
        vr = kb.s.cur("vE")
        kb.act(fA[:, FCS + b * 16:FCS + (b + 1) * 16],
               psA[:, PB3 + b * 32 + 16:PB3 + b * 32 + 32], AF.Exp, waits=[("vE", vr)])
        kb.act(fA[:, FMB + b * 16:FMB + (b + 1) * 16],
               fA[:, FR2 + b * 16:FR2 + (b + 1) * 16], AF.Exp, inc="sE")
        se = kb.s.cur("sE")
        kb.tss("vector", fA[:, FMB + b * 16:FMB + (b + 1) * 16],
               fA[:, FMB + b * 16:FMB + (b + 1) * 16], -1.0, ALU.mult, waits=[("sE", se)])
        kb.cp("vector", bA[0:16, LTH + b * 128:LTH + (b + 1) * 128],
              psA[0:16, PB_LT:PB_LT + 128])
        kb.tt("vector", bA[0:16, LTL + b * 128:LTL + (b + 1) * 128],
              psA[0:16, PB_LT:PB_LT + 128],
              bA[0:16, LTH + b * 128:LTH + (b + 1) * 128], ALU.subtract, inc="vE")
        kb.act(fA[0:16, FROW:FROW + 1], psA[0:16, PB_TOT:PB_TOT + 1], AF.Exp, inc="sE")
        sf = kb.s.cur("sE")
        kb.cp("vector", bA[0:16, SROWB:SROWB + 1], fA[0:16, FROW:FROW + 1],
              waits=[("sE", sf)], inc="vE")
        vf = kb.s.cur("vE")
        for q in range(8):
            kb.mm(psA[:, PB_FC + q:PB_FC + q + 1], T["sc_ind64"][:, q * 128:(q + 1) * 128],
                  bA[0:16, SROWB:SROWB + 1], start=True, stop=True,
                  waits=[("vE", vf)] if q == 0 else (),
                  inc="tE" if q == 7 else None)
        tf = kb.s.cur("tE")
        kb.cp("vector", fA[:, FFC + b * 8:FFC + (b + 1) * 8],
              psA[:, PB_FC:PB_FC + 8], waits=[("tE", tf)], inc="vE")
    lmach_done = kb.s.cur("vE")
    lmach_se = kb.s.cur("sE")

    if astop < 8:
        return kb.s.cur("vE")
    for b in range(BPC):
        for kt in range(NKT):
            kb.dmat(DMATQ,
                    bA[:, KTOK + b * 1024 + kt * 128:KTOK + b * 1024 + (kt + 1) * 128],
                    bA[:, KT + kt * TOKC + b * CT:KT + kt * TOKC + (b + 1) * CT],
                    waits=[("vE", k_norm_done)] if (b == 0 and kt == 0) else (),
                    inc="dRa")
    ktok_done = kb.s.cur("dRa")

    if astop < 9:
        return kb.s.cur("vE")
    ZP = {0: (2048, 2176, 3328, 2560, 2432, 2880), 1: (0, 128, 896, 512, 256, 832)}
    pair_ret = {}
    inst_guard = {}
    b0_guard = {}
    import os
    dbar = os.environ.get("DBAR")
    dstop = float(os.environ.get("DSTOP", "9"))
    _ym = int(os.environ.get("YMASK", "768"))
    def dinst(b, h):
        if True:
            i = b * H + h
            par = i % 2
            st = DSM + par * 1088
            GA, GG, REP, SQ, APP, B0R = ZP[par]
            hb = (h % 2) * 64
            ktc = KT + (h // 2) * TOKC + b * CT
            qtc = QT + (h // 2) * TOKC + b * CT
            scol = SOFF + (b * 8 + h // 2) * 64
            g = inst_guard.get(par)
            w0 = [("vE", k_norm_done), ("vE", lmach_done), ("sE", lmach_se),
                  ("sE", gate_done)]
            if G["psO_guard"]:
                w0.append(("sE", G["psO_guard"]))
            if g:
                w0 = w0 + [("vE", g[0]), ("sE", g[1])]
            kb.mm(psA[:, GA:GA + 128], bA[hb:hb + 64, ktc:ktc + CT],
                  bA[hb:hb + 64, ktc:ktc + CT], start=True, stop=True, waits=w0)
            kb.mm(psA[:, GG:GG + 128], bA[hb:hb + 64, ktc:ktc + CT],
                  bA[hb:hb + 64, qtc:qtc + CT], start=True, stop=True)
            kb.mm(psA[:, REP:REP + 128], T["sc_ind16"][:, h * 128:(h + 1) * 128],
                  bA[0:16, LTH + b * 128:LTH + (b + 1) * 128], start=True, stop=False)
            kb.mm(psA[:, REP:REP + 128], T["sc_ind16"][:, h * 128:(h + 1) * 128],
                  bA[0:16, LTL + b * 128:LTL + (b + 1) * 128], start=False, stop=True,
                  inc="tE")
            t_g = kb.s.cur("tE")
            if _ym & 1:
                yield
            if dstop < 2:
                inst_guard[par] = (kb.s.cur("vE"), kb.s.cur("sE"))
                return
            wE = [("tE", t_g)] + ([("vE", g[0])] if g else [])
            kb.op("vector", lambda e, o=psA[:, SQ:SQ + 128], i=psA[:, REP:REP + 128],
                  s1=fA[:, FR2 + b * 16 + h:FR2 + b * 16 + h + 1]:
                  e.tensor_scalar(out=o, in0=i, scalar1=s1, scalar2=0.0,
                                  op0=ALU.subtract, op1=ALU.max), waits=wE)
            kb.op("vector", lambda e, o=psA[:, SQ + 128:SQ + 256], i=psA[:, REP:REP + 128],
                  s1=fA[:, FL + b * 16 + h:FL + b * 16 + h + 1]:
                  e.tensor_scalar(out=o, in0=i, scalar1=s1, scalar2=0.0,
                                  op0=ALU.subtract, op1=ALU.min))
            v_arg = kb.s.cur("vE")
            if _ym & 2:
                yield
            kb.act(bA[:, st + D_EN:st + D_EN + 128], psA[:, SQ:SQ + 128], AF.Exp,
                   scale=-1.0, waits=[("vE", v_arg)])
            kb.act(bA[:, st + D_E2:st + D_E2 + 128], psA[:, SQ + 128:SQ + 256], AF.Exp)
            kb.act(bA[:, st + D_EL:st + D_EL + 128], psA[:, REP:REP + 128], AF.Exp,
                   inc="sE")
            s_e = kb.s.cur("sE")
            if _ym & 4:
                yield
            if dstop < 3:
                inst_guard[par] = (kb.s.cur("vE"), kb.s.cur("sE"))
                return
            kb.tt("vector", bA[:, st + D_N:st + D_N + 128], psA[:, GA:GA + 128],
                  bA[:, st + D_EN:st + D_EN + 128], ALU.mult, waits=[("sE", s_e)])
            kb.tt("vector", bA[:, st + D_N:st + D_N + 128], bA[:, st + D_N:st + D_N + 128],
                  T["sc_mN0"][:], ALU.mult)
            kb.tt("vector", bA[:, st + D_GM:st + D_GM + 128], psA[:, GG:GG + 128],
                  bA[:, st + D_E2:st + D_E2 + 128], ALU.mult)
            kb.tt("vector", bA[:, st + D_GM:st + D_GM + 128],
                  bA[:, st + D_GM:st + D_GM + 128], T["sc_mGm"][:], ALU.mult, inc="vE")
            v_b = kb.s.cur("vE")
            if _ym & 8:
                yield
            if dstop < 4:
                inst_guard[par] = (kb.s.cur("vE"), kb.s.cur("sE"))
                return
            kb.tt("vector", bA[hb:hb + 64, st + D_QD:st + D_QD + 128],
                  bA[hb:hb + 64, qtc:qtc + CT],
                  bA[hb:hb + 64, st + D_EL:st + D_EL + 128], ALU.mult)
            kb.tss("vector", bA[:, st + D_KC:st + D_KC + 64],
                   bA[:, KTOK + b * 1024 + h * 64:KTOK + b * 1024 + (h + 1) * 64],
                   fA[:, FCS + b * 16 + h:FCS + b * 16 + h + 1], ALU.mult,
                   waits=[("dRa", ktok_done)], inc="vE")
            v_kc = kb.s.cur("vE")
            if _ym & 16:
                yield
            if dstop < 5:
                inst_guard[par] = (kb.s.cur("vE"), kb.s.cur("sE"))
                return
            nsem = "dNa" if par == 0 else "dNb"
            wNT = [("vE", v_b)]
            if G["nt_cons"].get(par):
                wNT.append(("tE", G["nt_cons"][par]))
            kb.dmat(DMATQ, bA[:, st + D_NT:st + D_NT + 128],
                    bA[:, st + D_N:st + D_N + 128], waits=wNT, inc=nsem)
            s_nt_d = (nsem, kb.s.cur(nsem))
            if _ym & 32:
                yield
            if dstop < 6:
                inst_guard[par] = (kb.s.cur("vE"), kb.s.cur("sE"))
                return
            sg = G["sguard"].get((b, h // 2))
            wB = ([("vE", sg)] if sg else []) + \
                 ([("vE", b0_guard[par])] if par in b0_guard else [])
            kb.mm(psA[:, B0R:B0R + 64], bA[hb:hb + 64, ktc:ktc + CT],
                  bA[hb:hb + 64, scol:scol + 64], start=True, stop=True,
                  waits=wB, inc="tE")
            t_b0 = kb.s.cur("tE")
            if _ym & 64:
                yield
            if dstop < 6.5:
                inst_guard[par] = (kb.s.cur("vE"), kb.s.cur("sE"))
                return
            kb.stt("vector", bA[:, st + D_XA:st + D_XA + 64], psA[:, B0R:B0R + 64],
                   fA[:, FMB + b * 16 + h:FMB + b * 16 + h + 1],
                   bA[:, VV + b * 1024 + h * 64:VV + b * 1024 + (h + 1) * 64],
                   ALU.mult, ALU.add, waits=[("tE", t_b0), ("sE", v_done)], inc="vE")
            v_w = kb.s.cur("vE")
            b0_guard[par] = v_w
            if _ym & 128:
                yield
            if dstop < 7:
                inst_guard[par] = (kb.s.cur("vE"), kb.s.cur("sE"))
                return
            xc, xn_ = D_XA, D_XB
            ev_wait = s_nt_d
            v_app = v_w
            for k in range(7):
                kb.mm(psA[:, APP:APP + 64], bA[:, st + D_NT:st + D_NT + 128],
                      bA[:, st + xc:st + xc + 64], start=True, stop=True,
                      waits=[ev_wait, ("vE", v_app)], inc="tE")
                t_app = kb.s.cur("tE")
                if k < 6:
                    kb.mm(psA[:, SQ:SQ + 128], bA[:, st + D_NT:st + D_NT + 128],
                          bA[:, st + D_N:st + D_N + 128], start=True, stop=True)
                    kb.mm(psA[:, SQ + 128:SQ + 256], bA[:, st + D_N:st + D_N + 128],
                          bA[:, st + D_NT:st + D_NT + 128], start=True, stop=True,
                          inc="tE")
                    t_sq = kb.s.cur("tE")
                if _ym & 256:
                    yield
                kb.tt("vector", bA[:, st + xn_:st + xn_ + 64], bA[:, st + xc:st + xc + 64],
                      psA[:, APP:APP + 64], ALU.add, waits=[("tE", t_app)], inc="vE")
                v_app = kb.s.cur("vE")
                xc, xn_ = xn_, xc
                if k < 6:
                    kb.cp("scalar", bA[:, st + D_N:st + D_N + 256], psA[:, SQ:SQ + 256],
                          waits=[("tE", t_sq)], inc="sE")
                    ev_wait = ("sE", kb.s.cur("sE"))
                if _ym & 512:
                    yield
            U = st + xc
            G["nt_cons"][par] = kb.s.cur("tE")
            if dstop < 8:
                inst_guard[par] = (kb.s.cur("vE"), kb.s.cur("sE"))
                return
            wO = [("vE", v_app)] + ([("sE", G["psO_guard"])] if G["psO_guard"] else [])
            if h % 2 == 1 and "o" in pair_ret:
                wO.append(("tE", pair_ret["o"]))
            kb.mm(psA[hb:hb + 64, PB_O:PB_O + 128], bA[:, U:U + 64],
                  bA[:, st + D_GM:st + D_GM + 128], start=True, stop=False, waits=wO)
            kb.mm(psA[hb:hb + 64, PB_O:PB_O + 128], bA[hb:hb + 64, scol:scol + 64],
                  bA[hb:hb + 64, st + D_QD:st + D_QD + 128], start=False, stop=True,
                  inc="tE")
            t_o = kb.s.cur("tE")
            if h % 2 == 0:
                pair_ret["o"] = t_o
            if _ym & 1024:
                yield
            if dstop < 9:
                inst_guard[par] = (kb.s.cur("vE"), kb.s.cur("sE"))
                return
            wDS = [("vE", v_kc)] + ([("vE", G["ds_guard"])] if G["ds_guard"] else [])
            if h % 2 == 1 and "ds" in pair_ret:
                wDS.append(("tE", pair_ret["ds"]))
            kb.mm(psA[hb:hb + 64, PB_DS:PB_DS + 64], bA[:, st + D_KC:st + D_KC + 64],
                  bA[:, U:U + 64], start=True, stop=True, waits=wDS, inc="tE")
            t_ds = kb.s.cur("tE")
            if h % 2 == 0:
                pair_ret["ds"] = t_ds
            if _ym & 2048:
                yield
            if h % 2 == 1:
                kb.cp("scalar",
                      bA[:, OT + (h // 2) * TOKC + b * CT:OT + (h // 2) * TOKC + (b + 1) * CT],
                      psA[:, PB_O:PB_O + 128], waits=[("tE", t_o)], inc="sE")
                G["psO_guard"] = kb.s.cur("sE")
                kb.stt("vector", bA[:, scol:scol + 64], bA[:, scol:scol + 64],
                       fA[:, FFC + b * 8 + h // 2:FFC + b * 8 + h // 2 + 1],
                       psA[:, PB_DS:PB_DS + 64], ALU.mult, ALU.add,
                       waits=[("tE", t_ds)], inc="vE")
                G["sguard"][(b, h // 2)] = kb.s.cur("vE")
                G["ds_guard"] = kb.s.cur("vE")
            inst_guard[par] = (kb.s.cur("vE"), kb.s.cur("sE"))

    insts = [(b, h) for b in range(BPC) for h in range(H)]
    if _os.environ.get("DELTASEQ"):
        for bh in insts:
            if dbar:
                kb.barrier()
            for _ in dinst(*bh):
                pass
    else:
        for j0 in range(0, len(insts), 2):
            if dbar:
                kb.barrier()
            alive = [dinst(*insts[j0]), dinst(*insts[j0 + 1])]
            while alive:
                for gg in list(alive):
                    try:
                        next(gg)
                    except StopIteration:
                        alive.remove(gg)
    delta_done_v = kb.s.cur("vE")
    delta_done_s = kb.s.cur("sE")
    kb.tap(f"ot_{ch}", bA[:, OT:OT + 4096], [("sE", delta_done_s)])

    if astop < 10:
        return kb.s.cur("vE")
    for ct in range(NKT):
        kb.tt("vector", bA[:, OT + ct * TOKC:OT + (ct + 1) * TOKC],
              bA[:, OT + ct * TOKC:OT + (ct + 1) * TOKC],
              bA[:, GATE + ct * TOKC:GATE + (ct + 1) * TOKC], ALU.mult,
              waits=[("sE", max(delta_done_s, gate_done))] if ct == 0 else (),
              inc="vE" if ct == NKT - 1 else None)
    v_g = kb.s.cur("vE")
    gstop = int(os.environ.get("GSTOP", "3"))
    if gstop < 2:
        return kb.s.cur("vE")
    kb.fence("tensor", ["vE", "sE"])
    if os.environ.get("LNINNER", "1") == "0":
        for ct in range(NKT):
            kb.cp("vector", bA[:, GATE + ct * TOKC:GATE + (ct + 1) * TOKC],
                  bA[:, OT + ct * TOKC:OT + (ct + 1) * TOKC],
                  waits=[("vE", v_g)] if ct == 0 else (),
                  inc="vE" if ct == NKT - 1 else None)
        v_n = kb.s.cur("vE")
    else:
        rdy = [("vE", v_g)]
        for gg in G["pbank"]:
            if gg:
                rdy.append(gg)
        v_n = emit_ln(kb, OT, GATE, src_is_f32=False, ready=rdy, pbs=(PB0, PB1))

    def ev_o(ct, ps, tvv):
        kb.tt("vector", fA[:, xoff + ct * 512:xoff + (ct + 1) * 512], ps,
              fA[:, xoff + ct * 512:xoff + (ct + 1) * 512], ALU.add,
              waits=[("tE", tvv)], inc="vE")
        return ("vE", kb.s.cur("vE"))
    if gstop < 3:
        return kb.s.cur("vE")
    emit_proj(kb, s1, GATE, [("dWb", wo), ("vE", v_n)], ev_o, G)
    G["swb"] = [("tE", kb.s.cur("tE")), ("tE", kb.s.cur("tE"))]
    return kb.s.cur("vE")



def emit_ffn(kb, xoff, G, ready=()):
    T = kb.t
    bA, fA, psA = T["bA"], T["fA"], T["psA"]
    kb.fence("tensor", ["vE", "sE", "pE"])
    kb.fence("vector", ["tE", "sE", "pE"])
    vln = emit_ln(kb, xoff, XN, ready=ready)
    ZSL = (1536, 2048, 2560, 3072)
    for th in range(2):
        zguard = G["zslot"]
        gelu_p = [0] * 32
        for p in range(32):
            q = p % 4
            w = [("vE", vln)] if p == 0 else []
            for h in range(4):
                jt = p * 4 + h
                zg = zguard[q]
                kb.mm(psA[:, ZSL[q] + 0:ZSL[q] + 256],
                      bA[(q) * 32:(q + 1) * 32,
                         MALL + (p // 4) * 512 + h * 128:MALL + (p // 4) * 512 + (h + 1) * 128],
                      bA[(q) * 32:(q + 1) * 32,
                         XN + (p // 4) * TOKC + th * 256:XN + (p // 4) * TOKC + th * 256 + 256],
                      start=True, stop=True, tp=(q * 32, 0),
                      waits=w + ([("sE", zg)] if zg else []), inc="tE")
                tz = kb.s.cur("tE")
                kb.act(bA[:, HT + jt * 256:HT + (jt + 1) * 256],
                       psA[:, ZSL[q] + 0:ZSL[q] + 256], AF.Gelu, waits=[("tE", tz)],
                       inc="sE")
                zguard[q] = kb.s.cur("sE")
                w = []
            gelu_p[p] = kb.s.cur("sE")
        rails_done = kb.s.cur("sE")
        for g in range(32):
            slot = SWC + (G["wcn"] % 2) * 4096
            wg = G["swc"][G["wcn"] % 2]
            csem = "dCa" if G["wcn"] % 2 == 0 else "dCb"
            ld = kb.dma("sync", bA[:, slot:slot + 4096],
                        rr3(T["s_cperm"])[:, g * 4:(g + 1) * 4, :],
                        waits=([wg] if wg else []) + ([("dG", G["cast_done"])] if G["wcn"] == 0 else []),
                        inc=csem)
            for jj in range(4):
                js = g * 4 + jj
                for et in range(NKT):
                    w = ()
                    if jj == 0 and et == 0:
                        w = [(csem, ld), ("sE", gelu_p[g]), ("vE", G["ffn_ev"])]
                    elif g == 0 and jj == 0 and 3 <= et <= 6:
                        w = [("sE", zguard[et - 3])]
                    kb.mm(psA[:, et * 512:et * 512 + 256],
                          bA[:, slot + jj * 1024 + et * 128:slot + jj * 1024 + (et + 1) * 128],
                          bA[:, HT + js * 256:HT + (js + 1) * 256],
                          start=(g == 0 and jj == 0), stop=(g == 31 and jj == 3),
                          waits=w,
                          inc="tE" if (jj == 3 and et == NKT - 1) else None)
            G["swc"][G["wcn"] % 2] = ("tE", kb.s.cur("tE"))
            G["wcn"] += 1
        t_sweep = kb.s.cur("tE")
        for et in range(NKT):
            kb.tt("vector", fA[:, xoff + et * 512 + th * 256:xoff + et * 512 + th * 256 + 256],
                  psA[:, et * 512:et * 512 + 256],
                  fA[:, xoff + et * 512 + th * 256:xoff + et * 512 + th * 256 + 256],
                  ALU.add, waits=[("tE", t_sweep)] if et == 0 else (),
                  inc="vE" if et == NKT - 1 else None)
        G["ffn_ev"] = kb.s.cur("vE")
        kb.fence("tensor", ["vE"])
    G["ffn_sE"] = kb.s.cur("sE")
    return kb.s.cur("vE")



def emit_attn_b(kb, ch, xoff, G):
    import os
    T = kb.t
    bA, fA, psA = T["bA"], T["fA"], T["psA"]
    s0, s1 = SWB, SWB + 8192

    wg_ = kb.dma(WQQ, bA[:, s0:s0 + 8192], rr3(T["s_bg"]), waits=[G["swb"][0]], inc="dWa")
    wr_ = kb.dma(WQQ, bA[:, s1:s1 + 8192], rr3(T["s_br"]), waits=[G["swb"][1]], inc="dWb")

    vln = emit_ln(kb, xoff, XN,
                  ready=[("dXa" if ch % 2 == 0 else "dXb", G["xld"][ch]),
                         ("vE", G["mix_done"]),
                         ("vE", G["ffn_ev"]), ("sE", G.get("ffn_sE", 0))],
                  pad=True, sqoff=MIX)
    for kt in range(NKT):
        o = bA[:, XN + kt * 516:XN + kt * 516 + 516].rearrange(
            "p (b t) -> p b t", b=BPC)[:, :, 0:1]
        i = bA[:, SPREV + kt * 4:SPREV + (kt + 1) * 4].rearrange(
            "p (b t) -> p b t", b=BPC, t=1)
        kb.cp("vector", o, i, waits=[("vE", G["lead_v"])] if kt == 0 else (),
              inc="vE" if kt == NKT - 1 else None)
    lead_v = kb.s.cur("vE")

    for kt in range(NKT):
        for b in range(BPC):
            base = XN + kt * 516 + b * 129
            kb.tt("vector", bA[:, XBF + kt * 512 + b * 128:XBF + kt * 512 + (b + 1) * 128],
                  bA[:, base + 1:base + 129], bA[:, base:base + 128], ALU.subtract,
                  waits=[("vE", max(vln, lead_v))] if (kt == 0 and b == 0) else ())
    diff_done = kb.s.cur("vE")

    PIDX = {"g": 5, "r": 6, "k": 7, "v": 8}
    DSTS = {"g": GB, "r": RB, "k": KB2_, "v": VB2}
    mixbufs = {"g": MIX, "r": WKVB, "k": MIX, "v": WKVB}
    wslots = {"g": s0, "r": s1}
    projw = {"g": wg_, "r": wr_}
    prev_mm = {}
    for pi, pn in enumerate(("g", "r", "k", "v")):
        mb = mixbufs[pn]
        if os.environ.get("MIXBAR"):
            kb.barrier()
        mix_w = [("vE", diff_done)]
        if pi >= 2:
            mix_w.append(("tE", prev_mm[("g", "r", "k", "v")[pi - 2]]))
        for kt in range(NKT):
            for b in range(BPC):
                base = XN + kt * 516 + b * 129
                kb.stt("vector", bA[:, mb + kt * 512 + b * 128:mb + kt * 512 + (b + 1) * 128],
                       bA[:, XBF + kt * 512 + b * 128:XBF + kt * 512 + (b + 1) * 128],
                       fA[:, FCOL + kt * 11 + PIDX[pn]:FCOL + kt * 11 + PIDX[pn] + 1],
                       bA[:, base:base + 128], ALU.mult, ALU.add,
                       waits=mix_w if (kt == 0 and b == 0) else ())
        mx = kb.s.cur("vE")
        if pn == "k":
            projw["k"] = kb.dma(WQQ, bA[:, s0:s0 + 8192], rr3(T["s_bk"]),
                                waits=[("tE", prev_mm["g"])], inc="dWa")
            wslots["k"] = s0
        if pn == "v":
            projw["v"] = kb.dma(WQQ, bA[:, s1:s1 + 8192], rr3(T["s_bv"]),
                                waits=[("tE", prev_mm["r"])], inc="dWb")
            wslots["v"] = s1
        dst = DSTS[pn]
        bj = PIDX[pn] - 5
        for ct in range(NKT):
            pb = PB0 if ct % 2 == 0 else PB1
            for kt in range(NKT):
                wsem = {"g": "dWa", "r": "dWb", "k": "dWa", "v": "dWb"}[pn]
                w = [(wsem, projw[pn]), ("vE", mx)] if (ct == 0 and kt == 0) else []
                if kt == 0 and G["pbank"][ct % 2]:
                    w.append(G["pbank"][ct % 2])
                kb.mm(psA[:, pb:pb + TOKC],
                      bA[:, wslots[pn] + kt * 1024 + ct * 128:wslots[pn] + kt * 1024 + (ct + 1) * 128],
                      bA[:, mb + kt * TOKC:mb + (kt + 1) * TOKC],
                      start=(kt == 0), stop=(kt == NKT - 1), waits=w,
                      inc="tE" if kt == NKT - 1 else None)
            tvv = kb.s.cur("tE")
            bcol = fA[:, FCOL + ct * 11 + bj:FCOL + ct * 11 + bj + 1]
            od = bA[:, dst + ct * TOKC:dst + (ct + 1) * TOKC]
            if pn == "g":
                G["pbank"][ct % 2] = ("sE", kb.act(od, psA[:, pb:pb + TOKC], AF.Sigmoid,
                                                   bias=bcol, waits=[("tE", tvv)]))
            else:
                kb.tss("vector", od, psA[:, pb:pb + TOKC], bcol, ALU.add,
                       waits=[("tE", tvv)], inc="vE")
                G["pbank"][ct % 2] = ("vE", kb.s.cur("vE"))
        prev_mm[pn] = kb.s.cur("tE")
    proj_v = kb.s.cur("vE")
    proj_s = kb.s.cur("sE")
    G["mix_done"] = kb.s.cur("vE")

    if ch + 1 < NCH:
        for kt in range(NKT):
            o = bA[:, SPREV + kt * 4:SPREV + (kt + 1) * 4].rearrange(
                "p (b t) -> p b t", b=BPC, t=1)
            i = bA[:, XN + kt * 516:XN + kt * 516 + 516].rearrange(
                "p (b t) -> p b t", b=BPC)[:, :, 128:129]
            kb.cp("vector", o, i,
                  waits=[("vE", G["mix_done"])] if kt == 0 else ())
        G["lead_v"] = kb.s.cur("vE")

    for ct in range(NKT):
        kb.tt("vector", bA[:, KVB + ct * TOKC:KVB + (ct + 1) * TOKC],
              bA[:, KB2_ + ct * TOKC:KB2_ + (ct + 1) * TOKC],
              bA[:, VB2 + ct * TOKC:VB2 + (ct + 1) * TOKC], ALU.mult,
              waits=[("vE", proj_v)] if ct == 0 else (),
              inc="vE" if ct == NKT - 1 else None)
    v_kv = kb.s.cur("vE")
    for ct in range(NKT):
        for b in range(BPC):
            o = bA[:, ACCB + ct * TOKC + b * CT:ACCB + ct * TOKC + (b + 1) * CT]
            init = 0.0 if ch == 0 else bA[:, SPREV + 32 + ct * 4 + b:
                                          SPREV + 32 + ct * 4 + b + 1]
            kb.op("vector", lambda e, o=o, d0=fA[:, FDT + ct * 128:FDT + (ct + 1) * 128],
                  d1=bA[:, KVB + ct * TOKC + b * CT:KVB + ct * TOKC + (b + 1) * CT],
                  ini=init: e.tensor_tensor_scan(out=o, data0=d0, data1=d1, initial=ini,
                                                 op0=ALU.mult, op1=ALU.add),
                  waits=[("vE", v_kv)] if (ct == 0 and b == 0) else (),
                  inc="vE" if (ct == NKT - 1 and b == BPC - 1) else None)
    v_scan = kb.s.cur("vE")
    for ct in range(NKT):
        o = bA[:, SPREV + 32 + ct * 4:SPREV + 32 + (ct + 1) * 4].rearrange(
            "p (b t) -> p b t", b=BPC, t=1)
        i = bA[:, ACCB + ct * TOKC:ACCB + (ct + 1) * TOKC].rearrange(
            "p (b t) -> p b t", b=BPC)[:, :, 127:128]
        kb.cp("vector", o, i, inc="vE" if ct == NKT - 1 else None)
    for ct in range(NKT):
        kb.stt("vector", bA[:, WKVB + ct * TOKC:WKVB + (ct + 1) * TOKC],
               bA[:, KVB + ct * TOKC:KVB + (ct + 1) * TOKC],
               fA[:, FCOL + ct * 11 + 10:FCOL + ct * 11 + 11],
               bA[:, ACCB + ct * TOKC:ACCB + (ct + 1) * TOKC], ALU.mult, ALU.add,
               waits=[("vE", v_scan)] if ct == 0 else ())
        kb.tt("vector", bA[:, MIX + ct * TOKC:MIX + (ct + 1) * TOKC],
              bA[:, RB + ct * TOKC:RB + (ct + 1) * TOKC],
              bA[:, WKVB + ct * TOKC:WKVB + (ct + 1) * TOKC], ALU.mult)
        kb.tt("vector", bA[:, WKVB + ct * TOKC:WKVB + (ct + 1) * TOKC],
              bA[:, MIX + ct * TOKC:MIX + (ct + 1) * TOKC],
              bA[:, MIX + ct * TOKC:MIX + (ct + 1) * TOKC], ALU.mult,
              inc="vE" if ct == NKT - 1 else None)
    v_rw = kb.s.cur("vE")
    for ct in range(NKT):
        pb = PB0 if ct % 2 == 0 else PB1
        pbo = PB1 if ct % 2 == 0 else PB0
        w = [("vE", v_rw)] if ct == 0 else []
        for gg in G["pbank"]:
            if gg:
                w.append(gg)
        kb.mm(psA[0:2, pb:pb + 512], T["sc_bi64"][:, 2:4],
              bA[:, MIX + ct * TOKC:MIX + (ct + 1) * TOKC], start=True, stop=True,
              waits=w)
        kb.mm(psA[0:2, pbo:pbo + 512], T["sc_bi64"][:, 2:4],
              bA[:, WKVB + ct * TOKC:WKVB + (ct + 1) * TOKC], start=True, stop=True,
              inc="tE")
        tg = kb.s.cur("tE")
        kb.cp("vector", fA[0:2, FROW:FROW + 512], psA[0:2, pb:pb + 512], waits=[("tE", tg)])
        kb.cp("vector", fA[0:2, FROW + 512:FROW + 1024], psA[0:2, pbo:pbo + 512])
        kb.tt("vector", fA[0:2, FROW + 1024:FROW + 1536], fA[0:2, FROW:FROW + 512],
              fA[0:2, FROW:FROW + 512], ALU.mult)
        kb.tt("vector", fA[0:2, FROW + 512:FROW + 1024], fA[0:2, FROW + 512:FROW + 1024],
              fA[0:2, FROW + 1024:FROW + 1536], ALU.subtract)
        kb.cp("vector", bA[0:2, SROWB + 512:SROWB + 1024], fA[0:2, FROW:FROW + 512],
              inc="vE")
        vg = kb.s.cur("vE")
        kb.act(fA[0:2, FROW + 1024:FROW + 1536], fA[0:2, FROW + 512:FROW + 1024],
               AF.Ln, bias=EPS, waits=[("vE", vg)])
        kb.act(bA[0:2, SROWB:SROWB + 512], fA[0:2, FROW + 1024:FROW + 1536],
               AF.Exp, scale=-0.5, inc="sE")
        sg = kb.s.cur("sE")
        kb.mm(psA[:, pbo:pbo + 512], T["sc_rep64"][0:2, :], bA[0:2, SROWB:SROWB + 512],
              start=True, stop=True, waits=[("sE", sg)])
        kb.mm(psA[:, pb:pb + 512], T["sc_rep64"][0:2, :],
              bA[0:2, SROWB + 512:SROWB + 1024], start=True, stop=True, inc="tE")
        tg2 = kb.s.cur("tE")
        kb.tt("vector", bA[:, KVB + ct * TOKC:KVB + (ct + 1) * TOKC],
              bA[:, MIX + ct * TOKC:MIX + (ct + 1) * TOKC],
              psA[:, pb:pb + 512], ALU.subtract, waits=[("tE", tg2)])
        kb.tt("vector", bA[:, KVB + ct * TOKC:KVB + (ct + 1) * TOKC],
              bA[:, KVB + ct * TOKC:KVB + (ct + 1) * TOKC],
              psA[:, pbo:pbo + 512], ALU.mult)
        kb.tt("vector", bA[:, GB + ct * TOKC:GB + (ct + 1) * TOKC],
              bA[:, GB + ct * TOKC:GB + (ct + 1) * TOKC],
              bA[:, KVB + ct * TOKC:KVB + (ct + 1) * TOKC], ALU.mult,
              waits=[("sE", proj_s)] if ct == 0 else (), inc="vE")
        G["pbank"][ct % 2] = ("vE", kb.s.cur("vE"))
    v_pre = kb.s.cur("vE")
    wo_ = kb.dma(WQQ, bA[:, s0:s0 + 8192], rr3(T["s_bo"]),
                 waits=[("tE", prev_mm["k"])], inc="dWa")

    def ev_o(ct, ps, tvv):
        kb.stt("vector", fA[:, xoff + ct * 512:xoff + (ct + 1) * 512], ps,
               fA[:, FCOL + ct * 11 + 4:FCOL + ct * 11 + 5],
               fA[:, xoff + ct * 512:xoff + (ct + 1) * 512], ALU.add, ALU.add,
               waits=[("tE", tvv)], inc="vE")
        return ("vE", kb.s.cur("vE"))
    emit_proj(kb, s0, GB, [("dWa", wo_), ("vE", v_pre)], ev_o, G)
    G["swb"] = [("tE", kb.s.cur("tE")), ("tE", kb.s.cur("tE"))]
    return kb.s.cur("vE")



def build_full(dbg=(), p1=NCH, p2=NCH, sub=(1, 1, 1, 1)):
    nc = bass.Bass("TRN2", target_bir_lowering=False)
    sched = Sched()
    kb = KBX(nc, sched, dbg)
    T = kb.t

    def din(name, shape, dt=F32):
        T[name] = nc.dram_tensor(name, list(shape), dt, kind="ExternalInput").ap()

    din("xt", (D, NCH * TOKC))
    for nm in ("w_aq", "w_ak", "w_av", "w_ao"):
        din(nm, (D, D))
    din("w_anw", (D, 1)); din("w_afg", (D, 32)); din("w_ag1", (D, 64)); din("w_ag2", (64, D))
    for nm in ("w_bg", "w_br", "w_bk", "w_bv", "w_bo"):
        din(nm, (D, D))
    din("w_bbias", (D, 11))
    din("w_cperm", (JTOT, D))
    din("w_rm2", (128, 4096)); din("w_rup2", (128, 4096))
    din("c_mcum", (128, 128)); din("c_mrev", (128, 128))
    din("c_i128b", (128, 128), BF16); din("c_mN0", (128, 128), BF16)
    din("c_mGm", (128, 128), BF16)
    din("c_ones1x128b", (1, 128), BF16); din("c_meancol", (128, 1), BF16)
    din("c_onescol", (128, 1)); din("c_ind16", (16, 2048), BF16)
    din("c_ind64", (16, 1024), BF16); din("c_bi64", (128, 4), BF16)
    din("c_rep64", (4, 128), BF16); din("c_scal", (128, 8))
    T["outt"] = nc.dram_tensor("outt", [D, NCH * TOKC], F32, kind="ExternalOutput").ap()
    T["x2t"] = nc.dram_tensor("x2t", [D, NCH * TOKC], F32, kind="Internal").ap()
    for nm in ("s_aq", "s_ak", "s_av", "s_ao", "s_bg", "s_br", "s_bk", "s_bv", "s_bo"):
        T[nm] = nc.dram_tensor(nm, [D, D], BF16, kind="Internal").ap()
    T["s_cperm"] = nc.dram_tensor("s_cperm", [JTOT, D], BF16, kind="Internal").ap()
    for nm in dbg:
        shp = [D, NCH * TOKC] if nm == "x2t" else [128, 4096]
        kb.dbg_t[nm] = nc.dram_tensor("dbg_" + nm, shp, F32,
                                      kind="ExternalOutput").ap()

    import contextlib
    with contextlib.ExitStack() as stack:
        ent = stack.enter_context
        bA_h = ent(nc.sbuf_tensor("bA", [128, ABF], BF16))
        fA_h = ent(nc.sbuf_tensor("fA", [128, AF32], F32))
        consts = {}
        for nm, shape, dt in (("sc_mcum", [128, 128], F32), ("sc_mrev", [128, 128], F32),
                              ("sc_i128b", [128, 128], BF16), ("sc_mN0", [128, 128], BF16),
                              ("sc_mGm", [128, 128], BF16), ("sc_ones1x128b", [1, 128], BF16),
                              ("sc_meancol", [128, 1], BF16), ("sc_onescol", [128, 1], F32),
                              ("sc_ind16", [16, 2048], BF16), ("sc_ind64", [16, 1024], BF16),
                              ("sc_bi64", [128, 4], BF16), ("sc_rep64", [4, 128], BF16),
                              ("safg", [128, 256], BF16), ("sag1", [128, 512], BF16),
                              ("sag2", [64, 1024], BF16),
                              ("st1", [64, 512], BF16), ("sc_scal", [128, 8], F32)):
            consts[nm] = ent(nc.sbuf_tensor(nm, shape, dt))
        psA_h = ent(nc.psum_tensor("psA", [128, 4096], F32))
        sems = {}
        for snm in DMASEMS + ("tE", "vE", "sE", "pE"):
            sems[snm] = ent(nc.semaphore(snm))
        block = ent(nc.Block())
        T.update(consts)
        T["bA"] = bA_h
        T["fA"] = fA_h
        T["psA"] = psA_h
        bA, fA = T["bA"], T["fA"]

        sc = T["sc_scal"]
        for i, v in enumerate([EPS, 1.0, 1e-24, -0.5, -1.0, 0.5, 2.0, 0.0]):
            nc.const_aps.register(F32, float(v), sc[:, i:i + 1])
        G = {"swb": [None, None], "swc": [None, None], "wcn": 0, "pbank": [None, None],
             "ffn_sE": 0,
             "zslot": [None] * 4, "ffn_ev": 0, "sguard": {}, "psO_guard": 0,
             "ds_guard": 0, "nt_cons": {},
             "xld": {}, "xstore": {}, "mix_done": 0, "lead_v": 0}

        import os
        pret_v = emit_pretransform(kb, frozenset(os.environ.get("PSTAGES", "abcdefg")))
        G["cast_done"] = kb.s.cur("dG")
        G["ao_ready"] = kb.s.cur("dAo")
        G["swb"] = [("dAo", G["ao_ready"]), ("dAo", G["ao_ready"])]

        xt3 = rr3(T["xt"])
        x23 = rr3(T["x2t"])
        ot3 = rr3(T["outt"])

        def xsem(c):
            return "dXa" if c % 2 == 0 else "dXb"

        def ssem(c):
            return "dSa" if c % 2 == 0 else "dSb"

        G["xld"][0] = kb.dma("scalar", fA[:, FXT:FXT + 4096],
                             xt3[:, :, 0:TOKC], waits=[("dAo", G["ao_ready"])], inc="dXa")
        for ch in range(p1):
            xoff = FXT + (ch % 2) * 4096
            if ch + 1 < p1:
                w = [("dAo", G["ao_ready"])] if ch == 0 else \
                    [(ssem(ch - 1), G["xstore"].get(ch - 1, 0))]
                G["xld"][ch + 1] = kb.dma(
                    "scalar", fA[:, FXT + ((ch + 1) % 2) * 4096:FXT + ((ch + 1) % 2) * 4096 + 4096],
                    xt3[:, :, (ch + 1) * TOKC:(ch + 2) * TOKC], waits=w, inc=xsem(ch + 1))
            kb.barrier()
            if sub[0]:
                emit_attn_a(kb, ch, xoff, G)
            kb.tap(f"x1_{ch}", fA[:, xoff:xoff + 4096], [("vE", kb.s.cur("vE"))])
            v2 = kb.s.cur("vE")
            if sub[1]:
                kb.barrier()
                v2 = emit_ffn(kb, xoff, G,
                              ready=[("dXa" if ch % 2 == 0 else "dXb", G["xld"][ch])])
            kb.tap(f"x2_{ch}", fA[:, xoff:xoff + 4096], [("vE", v2)])
            G["xstore"][ch] = kb.dma("scalar", x23[:, :, ch * TOKC:(ch + 1) * TOKC],
                                     fA[:, xoff:xoff + 4096],
                                     waits=[("vE", v2)], inc=ssem(ch))

        if "x2t" in dbg:
            kb.dma("gpsimd", kb.dbg_t["x2t"][:, 0:p1 * TOKC], T["x2t"][:, 0:p1 * TOKC],
                   waits=[("dSa", kb.s.cur("dSa")), ("dSb", kb.s.cur("dSb"))], inc="dT")
        if p2 == 0:
            kb.fence("sync", list(DMASEMS), hard=True)
            _emit(sched, sems, block)
            return nc
        wst = G["xstore"][p1 - 1]
        for b in range(BPC):
            col = (p1 - 1) * TOKC + b * CT + CT - 1
            kb.dma("scalar",
                   fA[:, FLS + 640:FLS + 672].rearrange("p (k b) -> p k b", b=BPC)[:, :, b:b + 1],
                   x23[:, :, col:col + 1],
                   waits=[(ssem(p1 - 1), wst)] if b == 0 else [], inc="dWr", slow=True)
        wld = kb.s.cur("dWr")
        G["lead_v"] = emit_ln(kb, FLS + 640, SPREV, ncols=4, dstw=4,
                              ready=[("dWr", wld), ("vE", G["ffn_ev"]),
                                     ("sE", G.get("ffn_sE", 0))])

        G["xld"] = {}
        ws0 = G["xstore"].get(p1 - 2, 0)
        G["xld"][0] = kb.dma("scalar", fA[:, FXT:FXT + 4096], x23[:, :, 0:TOKC],
                             waits=[(ssem(p1 - 2), ws0)] if ws0 else [], inc="dXa")
        G["xstore"] = {}
        for ch in range(p2):
            xoff = FXT + (ch % 2) * 4096
            if ch + 1 < p2:
                w = [(ssem(ch - 1), G["xstore"].get(ch - 1, 0))] if ch >= 1 else []
                G["xld"][ch + 1] = kb.dma(
                    "scalar", fA[:, FXT + ((ch + 1) % 2) * 4096:FXT + ((ch + 1) % 2) * 4096 + 4096],
                    x23[:, :, (ch + 1) * TOKC:(ch + 2) * TOKC], waits=w, inc=xsem(ch + 1))
            kb.barrier()
            if sub[2]:
                emit_attn_b(kb, ch, xoff, G)
            kb.tap(f"x3_{ch}", fA[:, xoff:xoff + 4096], [("vE", kb.s.cur("vE"))])
            v2 = kb.s.cur("vE")
            if sub[3]:
                kb.barrier()
                v2 = emit_ffn(kb, xoff, G,
                              ready=[("dXa" if ch % 2 == 0 else "dXb", G["xld"][ch])])
            G["xstore"][ch] = kb.dma("scalar", ot3[:, :, ch * TOKC:(ch + 1) * TOKC],
                                     fA[:, xoff:xoff + 4096],
                                     waits=[("vE", v2)], inc=ssem(ch))

        kb.fence("sync", list(DMASEMS), hard=True)
        _emit(sched, sems, block)
    return nc


def _emit(sched, sems, block):
    def body(eng_name):
        def run(eng):
            for w, fn, inc, amt in sched.ops[eng_name]:
                for sem, val in w:
                    eng.wait_ge(sems[sem], val)
                if fn is None:
                    continue
                ins = fn(eng)
                if inc is not None:
                    ins.then_inc(sems[inc], amt)
        return run
    block.sync(body("sync"))
    block.tensor(body("tensor"))
    block.vector(body("vector"))
    block.scalar(body("scalar"))
    block.gpsimd(body("gpsimd"))


_CACHE = {}


def kernel(**inputs):
    in_maps = host_inputs(inputs)
    if "nc" not in _CACHE:
        _CACHE["nc"] = build_full()
    nc = _CACHE["nc"]
    res = run_bass_kernel_spmd(nc, in_maps, core_ids=list(range(NCORES)))
    return host_output(res.results)
